# version 16
# baseline (speedup 1.0000x reference)
import contextlib

import numpy as np

import concourse.bass as bass
import concourse.mybir as mybir
from concourse.bass_utils import run_bass_kernel_spmd

F32 = mybir.dt.float32
BF16 = mybir.dt.bfloat16
I32 = mybir.dt.int32
AF = mybir.ActivationFunctionType
ALU = mybir.AluOpType
AX = mybir.AxisListType

NCORES = 8

FULL = dict(tok=1024, d=2048, heads=16, hd=128, kv=4096, nexp=16, ef=1024,
            cap=192, shef=2048, qt=256)

ENGS = ("sync", "scalar", "vector", "tensor", "gpsimd")


class Op:
    __slots__ = ("eng", "fn", "deps", "dma", "needed", "val", "waits", "tag",
                 "slot")

    def __init__(self, eng, fn, dma, tag):
        self.eng = eng
        self.fn = fn
        self.deps = []
        self.dma = dma
        self.needed = False
        self.val = None
        self.waits = []
        self.tag = tag
        self.slot = 0

    @property
    def semkey(self):
        return f"{self.eng}_dma{self.slot}" if self.dma else self.eng


class Buf:
    __slots__ = ("name", "writer", "readers")

    def __init__(self, name):
        self.name = name
        self.writer = None
        self.readers = []


class View:
    __slots__ = ("ap", "bufs")

    def __init__(self, ap, bufs):
        self.ap = ap
        self.bufs = list(bufs)


class Slab:

    def __init__(self, name, ap, blk_cols, ncols, nrows=128):
        self.name = name
        self.t_ap = ap
        self.blk = blk_cols
        self.ncols = ncols
        self.nrows = nrows
        nblk = (ncols + blk_cols - 1) // blk_cols
        self.bufs = [Buf(f"{name}[{i}]") for i in range(nblk)]

    def v(self, c0, nc, r0=0, nr=None):
        nr = self.nrows if nr is None else nr
        assert 0 <= c0 and c0 + nc <= self.ncols, (self.name, c0, nc)
        b0 = c0 // self.blk
        b1 = (c0 + nc - 1) // self.blk
        return View(self.t_ap[r0:r0 + nr, c0:c0 + nc], self.bufs[b0:b1 + 1])


DMA_SLOTS = {"sync": 8, "scalar": 4, "vector": 4, "tensor": 4, "gpsimd": 4}


class KB:
    def __init__(self, nc, es):
        self.nc = nc
        self.es = es
        self.ops = {e: [] for e in ENGS}
        self.sems = {}
        self.dmacnt = {e: 0 for e in ENGS}
        self.lastdma = {}

    def op(self, eng, fn, r=(), w=(), dep=(), dma=False, tag=""):
        o = Op(eng, fn, dma, tag)
        if dma:
            o.slot = self.dmacnt[eng] % DMA_SLOTS[eng]
            self.dmacnt[eng] += 1
            prev = self.lastdma.get(o.semkey)
            if prev is not None:
                dep = list(dep) + [prev]
            self.lastdma[o.semkey] = o
        deps = []
        for vw in r:
            for b in vw.bufs:
                if b.writer is not None:
                    deps.append(b.writer)
        for vw in w:
            for b in vw.bufs:
                if b.writer is not None:
                    deps.append(b.writer)
                deps.extend(b.readers)
        deps.extend(dep)
        seen = set()
        for d in deps:
            if d is o or id(d) in seen:
                continue
            if d.eng == "tensor" and eng == "tensor" and not d.dma and not dma:
                continue
            seen.add(id(d))
            o.deps.append(d)
        for vw in r:
            for b in vw.bufs:
                b.readers.append(o)
        for vw in w:
            for b in vw.bufs:
                b.writer = o
                b.readers = []
        self.ops[eng].append(o)
        return o

    def generate(self):
        for e in ENGS:
            for o in self.ops[e]:
                for d in o.deps:
                    d.needed = True
        counters = {}
        for e in ENGS:
            for o in self.ops[e]:
                if o.needed:
                    k = o.semkey
                    counters[k] = counters.get(k, 0) + (16 if o.dma else 1)
                    o.val = counters[k]
        for k in counters:
            self.sems[k] = self.es.enter_context(self.nc.semaphore("S" + k))
        water = {e: {} for e in ENGS}
        for e in ENGS:
            for o in self.ops[e]:
                need = {}
                for d in o.deps:
                    k = d.semkey
                    if d.val > need.get(k, 0):
                        need[k] = d.val
                for k, v in sorted(need.items()):
                    if v > water[e].get(k, 0):
                        o.waits.append((k, v))
                        water[e][k] = v
        nc = self.nc
        with nc.Block() as blk:
            for engname in ENGS:
                ops = self.ops[engname]

                def runner(eng, ops=ops, self=self):
                    for o in ops:
                        for (k, v) in o.waits:
                            eng.wait_ge(self.sems[k], v)
                        if o.fn is not None:
                            inst = o.fn(eng)
                            if o.needed:
                                inst.then_inc(self.sems[o.semkey],
                                              16 if o.dma else 1)
                getattr(blk, engname)(runner)


def fm(a):
    r, c = a.shape
    assert r % 128 == 0
    return np.ascontiguousarray(
        a.reshape(r // 128, 128, c).transpose(1, 0, 2).reshape(128, -1))


def unfm(a, r, c):
    return np.ascontiguousarray(
        a.reshape(128, r // 128, c).transpose(1, 0, 2).reshape(r, c))


def pack_proj(wT, ocg_cols):
    k, o = wT.shape
    kc = k // 128
    ogn = o // ocg_cols
    blocks = []
    for og in range(ogn):
        for r in range(kc):
            blocks.append(wT[r * 128:(r + 1) * 128,
                             og * ocg_cols:(og + 1) * ocg_cols])
    return np.ascontiguousarray(np.concatenate(blocks, axis=1))


def build(cfg, stop_after=None):
    tok, d, heads = cfg["tok"], cfg["d"], cfg["heads"]
    kv, nexp, ef, cap = cfg["kv"], cfg["nexp"], cfg["ef"], cfg["cap"]
    shef, qt = cfg["shef"], cfg["qt"]
    KC = d // 128
    TC = tok // 128
    NQ = tok // qt
    KVC = kv // 128
    EFC = ef // 128
    SHC = shef // 128
    NQ2 = tok // 512
    KQ = kv // 4
    KQC = KQ // 128
    CAPM = max(cap, 128)
    SCALE = float(np.float32(1.0 / np.sqrt(np.float64(cfg["hd"]))))
    EPS = 1e-5
    CCS = [(0, min(128, cap))] + ([(128, cap - 128)] if cap > 128 else [])
    OCG = 2

    nc = bass.Bass("TRN2", target_bir_lowering=False, debug=False)

    dr = {}
    dr["xT"] = nc.dram_tensor("xT", [128, KC * tok], F32, kind="ExternalInput").ap()
    dr["wqT"] = nc.dram_tensor("wqT", [128, KC * d], F32, kind="ExternalInput").ap()
    dr["woT"] = nc.dram_tensor("woT", [128, KC * d], F32, kind="ExternalInput").ap()
    dr["kT"] = nc.dram_tensor("kT", [128, heads * kv], F32, kind="ExternalInput").ap()
    dr["vv"] = nc.dram_tensor("vv", [128, heads * kv], F32, kind="ExternalInput").ap()
    dr["rwT"] = nc.dram_tensor("rwT", [128, KC * nexp], F32, kind="ExternalInput").ap()
    dr["rb"] = nc.dram_tensor("rb", [1, nexp], F32, kind="ExternalInput").ap()
    dr["wu"] = nc.dram_tensor("wu", [nexp, 128, KC * ef], F32, kind="ExternalInput").ap()
    dr["wg"] = nc.dram_tensor("wg", [nexp, 128, KC * ef], F32, kind="ExternalInput").ap()
    dr["wd"] = nc.dram_tensor("wd", [nexp, 128, EFC * d], F32, kind="ExternalInput").ap()
    dr["susg"] = nc.dram_tensor("susg", [128, SHC * KC * 256], F32, kind="ExternalInput").ap()
    dr["sdT"] = nc.dram_tensor("sdT", [128, SHC * d], F32, kind="ExternalInput").ap()
    dr["yT"] = nc.dram_tensor("yT", [128, KC * tok], F32, kind="ExternalOutput").ap()
    dr["tms"] = nc.dram_tensor("tms", [128, 2 * TC * d], BF16, kind="Internal").ap()

    with contextlib.ExitStack() as es:
        E = es.enter_context
        kb = KB(nc, es)

        def sb(name, shape, dt):
            return E(nc.sbuf_tensor(name, shape, dt)).ap()

        xT = Slab("xT", sb("s_xT", [128, KC * tok], F32), 512, KC * tok)
        FARN = 11264
        FAR = Slab("FAR", sb("s_FAR", [128, FARN], F32), 512, FARN)

        A = KC * qt
        WPN = KC * OCG * 128
        QUAD = min(4, KVC)
        KQB = QUAD * 128
        NPSLOT = 3
        XR = max(2 * WPN, 8 * KQB + NPSLOT * 2 * QUAD * qt)
        attn_need = 6 * A + XR
        ln2_need = 2 * TC * d + 2 * tok
        shared_need = 2 * A + 2 * SHC * qt + 2 * KC * 256 + 2 * WPN
        exp_need = 2 * TC * d + 2 * 4 * 512
        BARN = max(attn_need, ln2_need, shared_need, exp_need)
        BAR = Slab("BAR", sb("s_BAR", [128, BARN], BF16), 512, BARN)

        LNH, LNL, QH, QL, AOH, AOL = (i * A for i in range(6))
        WPH, WPL = 6 * A, 6 * A + WPN
        KHQ, KLQ, VHQ, VLQ = (6 * A + i * 2 * KQB for i in range(4))
        PB = 6 * A + 8 * KQB
        TMH, TML, PCH, PCL = 0, TC * d, 2 * TC * d, 2 * TC * d + tok
        FMH, FML = 0, A
        HSH, HSL = 2 * A, 2 * A + SHC * qt
        SGH, SGL = 2 * A + 2 * SHC * qt, 2 * A + 2 * SHC * qt + KC * 256
        EOS = 2 * TC * d

        ioti = Slab("ioti", sb("s_ioti", [128, CAPM], I32), CAPM, CAPM)
        iotci = Slab("iotci", sb("s_iotci", [128, 1], I32), 1, 1)
        iotf = Slab("iotf", sb("s_iotf", [128, CAPM], F32), CAPM, CAPM)
        iotc = Slab("iotc", sb("s_iotc", [128, 1], F32), 1, 1)
        identf = Slab("identf", sb("s_identf", [128, 128], F32), 128, 128)
        identb = Slab("identb", sb("s_identb", [128, 128], BF16), 128, 128)
        tri2 = Slab("tri2", sb("s_tri2", [128, 128], F32), 128, 128)
        ones2d = Slab("ones2d", sb("s_ones2d", [128, 128], F32), 128, 128)
        onesc = Slab("onesc", sb("s_onesc", [128, 1], F32), 1, 1)
        onescb = Slab("onescb", sb("s_onescb", [128, 1], BF16), 1, 1)
        onesr = Slab("onesr", sb("s_onesr", [1, 128], F32), 128, 128, nrows=1)

        rows = Slab("rows", sb("s_rows", [1, 3072], F32), 512, 3072, nrows=1)
        cols4 = Slab("cols4", sb("s_cols4", [128, 4], F32), 1, 4)
        rwTs = Slab("rwTs", sb("s_rwTs", [128, KC * nexp], F32), 512, KC * nexp)
        rbrow = Slab("rbrow", sb("s_rbrow", [1, nexp], F32), nexp, nexp, nrows=1)
        biasbc = Slab("biasbc", sb("s_biasbc", [128, nexp], F32), nexp, nexp)
        logtm = Slab("logtm", sb("s_logtm", [128, TC * nexp], F32), nexp, TC * nexp)
        m12 = Slab("m12", sb("s_m12", [128, TC * nexp], F32), nexp, TC * nexp)
        cmbs = Slab("cmbs", sb("s_cmbs", [128, TC * nexp], F32), nexp, TC * nexp)
        cmbh = Slab("cmbh", sb("s_cmbh", [128, TC * nexp], BF16), nexp, TC * nexp)
        cmbl = Slab("cmbl", sb("s_cmbl", [128, TC * nexp], BF16), nexp, TC * nexp)
        cumcs = Slab("cumcs", sb("s_cumcs", [128, TC * nexp], F32), nexp, TC * nexp)
        slotsS = Slab("slots", sb("s_slots", [128, TC * nexp], F32), nexp, TC * nexp)
        smsk = Slab("smsk", sb("s_smsk", [128, 5 * nexp], F32), nexp, 5 * nexp)
        sbfS = Slab("sbf", sb("s_sbf", [128, TC * cap], BF16), cap, TC * cap)
        stm = Slab("stm", sb("s_stm", [128, len(CCS) * tok], BF16), 512, len(CCS) * tok)
        cmbcol = Slab("cmbcol", sb("s_cmbcol", [128, len(CCS)], F32), 1, len(CCS))

        PS = Slab("PS", E(nc.psum_tensor("p_PS", [128, 3072], F32)).ap(), 512, 3072)
        PT = Slab("PT", E(nc.psum_tensor("p_PT", [128, 2048], BF16)).ap(), 1024, 2048)

        def bank(b, cols=512, r0=0, nr=128):
            return PS.v(b * 512, cols, r0, nr)

        SCA, SCB = 0, 1024
        PF = [2048, 2560]
        RBF, SCC = 3072, 3584
        KARR, VARR, ARR = 4096, 6144, 8192
        FH, FU = 2048, 2048 + EFC * cap
        FGT = FU + EFC * cap
        assert FGT + KC * cap <= ARR, (FGT + KC * cap, ARR)
        assert ARR + 3 * 1024 <= FARN

        def mm(psv, statv, movv, start, stop, dep=()):
            return kb.op("tensor", lambda T, o=psv.ap, l=statv.ap, r=movv.ap,
                         a=start, b=stop: T.matmul(o, l, r, start=a, stop=b),
                         r=[statv, movv], w=[psv], dep=dep, tag="mm")

        def mm3(psv, sh, sl, mh, ml, start, stop):
            mm(psv, sh, mh, start, False)
            mm(psv, sh, ml, False, False)
            mm(psv, sl, mh, False, stop)

        def vcopy(eng, dst, src, dep=()):
            kb.op(eng, lambda Eo, o=dst.ap, i=src.ap: Eo.tensor_copy(out=o, in_=i),
                  r=[src], w=[dst], dep=dep, tag="copy")

        def scopy(dst, src, dep=()):
            kb.op("scalar", lambda Eo, o=dst.ap, i=src.ap: Eo.copy(o, i),
                  r=[src], w=[dst], dep=dep, tag="scopy")

        def tt(eng, dst, a, b, opr, dep=()):
            kb.op(eng, lambda Eo, o=dst.ap, x=a.ap, y=b.ap, p=opr:
                  Eo.tensor_tensor(out=o, in0=x, in1=y, op=p),
                  r=[a, b], w=[dst], dep=dep, tag="tt")

        def tss(eng, dst, a, scal, opr, dep=()):
            if isinstance(scal, View):
                kb.op(eng, lambda Eo, o=dst.ap, x=a.ap, s=scal.ap, p=opr:
                      Eo.tensor_single_scalar(out=o, in_=x, scalar=s, op=p),
                      r=[a, scal], w=[dst], dep=dep, tag="tss")
            else:
                kb.op(eng, lambda Eo, o=dst.ap, x=a.ap, s=scal, p=opr:
                      Eo.tensor_single_scalar(out=o, in_=x, scalar=s, op=p),
                      r=[a], w=[dst], dep=dep, tag="tss")

        def stt(eng, dst, in0, scal, in1, op0, op1, dep=()):
            sc = scal.ap if isinstance(scal, View) else scal
            rr = [in0, in1] + ([scal] if isinstance(scal, View) else [])
            kb.op(eng, lambda Eo, o=dst.ap, x=in0.ap, s=sc, y=in1.ap,
                  p0=op0, p1=op1:
                  Eo.scalar_tensor_tensor(out=o, in0=x, scalar=s, in1=y,
                                          op0=p0, op1=p1),
                  r=rr, w=[dst], dep=dep, tag="stt")

        def act(dst, src, func, scale=1.0, bias=0.0, dep=()):
            bap = bias.ap if isinstance(bias, View) else bias
            rr = [src] + ([bias] if isinstance(bias, View) else [])
            kb.op("scalar", lambda Eo, o=dst.ap, i=src.ap, f=func, s=scale,
                  b=bap: Eo.activation(o, i, f, bias=b, scale=s),
                  r=rr, w=[dst], dep=dep, tag="act")

        def red(dst, src, opr, dep=()):
            kb.op("vector", lambda Eo, o=dst.ap, i=src.ap, p=opr:
                  Eo.tensor_reduce(out=o, in_=i, axis=AX.X, op=p),
                  r=[src], w=[dst], dep=dep, tag="red")

        def recip(dst, src, dep=()):
            kb.op("vector", lambda Eo, o=dst.ap, i=src.ap:
                  Eo.reciprocal(out=o, in_=i), r=[src], w=[dst], dep=dep,
                  tag="recip")

        def memset(eng, dst, val, dep=()):
            kb.op(eng, lambda Eo, o=dst.ap, v=val: Eo.memset(o, v),
                  r=[], w=[dst], dep=dep, tag="memset")

        def dma(dst, src_ap, eng="sync", dep=()):
            return kb.op(eng, lambda Eo, o=dst.ap, i=src_ap:
                         Eo.dma_start(out=o, in_=i),
                         r=[], w=[dst], dep=dep, dma=True, tag="dma")

        def dma_out(dst_ap, src, eng="scalar", dep=()):
            return kb.op(eng, lambda Eo, o=dst_ap, i=src.ap:
                         Eo.dma_start(out=o, in_=i),
                         r=[src], w=[], dep=dep, dma=True, tag="dmao")

        def transpose(psv, src, ident, dep=()):
            kb.op("tensor", lambda T, o=psv.ap, i=src.ap, dd=ident.ap:
                  T.transpose(o, i, dd), r=[src, ident], w=[psv], dep=dep,
                  tag="tr")

        def split_pair(src, dsth, dstl, hi_eng="scalar", lo_eng="vector"):
            if hi_eng == "scalar":
                scopy(dsth, src)
            elif hi_eng == "dma":
                kb.op("gpsimd", lambda Eo, o=dsth.ap, i=src.ap:
                      Eo.dma_start(out=o, in_=i),
                      r=[src], w=[dsth], dma=True, tag="cdma")
            else:
                vcopy(hi_eng, dsth, src)
            tt(lo_eng, dstl, src, dsth, ALU.subtract)

        memset("vector", onesc.v(0, 1), 1.0)
        memset("vector", onescb.v(0, 1), 1.0)
        memset("vector", onesr.v(0, 128), 1.0)
        memset("vector", ones2d.v(0, 128), 1.0)
        kb.op("gpsimd", lambda G, o=iotci.v(0, 1).ap:
              G.iota(o, pattern=[[0, 1]], channel_multiplier=1),
              w=[iotci.v(0, 1)], tag="iota")
        kb.op("gpsimd", lambda G, o=ioti.v(0, CAPM).ap:
              G.iota(o, pattern=[[1, CAPM]], channel_multiplier=0),
              w=[ioti.v(0, CAPM)], tag="iota")
        vcopy("vector", iotc.v(0, 1), iotci.v(0, 1))
        vcopy("vector", iotf.v(0, CAPM), ioti.v(0, CAPM))
        tss("vector", identf.v(0, 128), iotf.v(0, 128), iotc.v(0, 1), ALU.is_equal)
        tss("vector", identb.v(0, 128), iotf.v(0, 128), iotc.v(0, 1), ALU.is_equal)
        tss("vector", tri2.v(0, 128), iotf.v(0, 128), iotc.v(0, 1), ALU.is_ge)
        dma(rwTs.v(0, KC * nexp), dr["rwT"][:, :])
        dma(rbrow.v(0, nexp), dr["rb"][:, :])
        mm(bank(0, nexp), onesr.v(0, 128), rbrow.v(0, nexp), True, True)
        vcopy("vector", biasbc.v(0, nexp), bank(0, nexp))
        XSTEP = min(4096, KC * tok)
        for c in range(KC * tok // XSTEP):
            dma(xT.v(c * XSTEP, XSTEP), dr["xT"][:, c * XSTEP:(c + 1) * XSTEP],
                eng="scalar")

        def ln_stats(mean_r, rstd_r):
            s1 = [bank(i, 512, nr=1) for i in range(NQ2)]
            s2 = [bank(NQ2 + i, 512, nr=1) for i in range(NQ2)]
            for kc in range(KC):
                sq0 = SCA if kc % 2 == 0 else SCB
                for q2 in range(NQ2):
                    xv = xT.v(kc * tok + q2 * 512, 512)
                    sqv = FAR.v(sq0 + (q2 % 2) * 512, 512)
                    act(sqv, xv, AF.Square)
                    mm(s1[q2], onesc.v(0, 1), xv, kc == 0, kc == KC - 1)
                    mm(s2[q2], onesc.v(0, 1), sqv, kc == 0, kc == KC - 1)
            r_s1 = FAR.v(KARR, tok, nr=1)
            r_s2 = FAR.v(KARR + 1024, tok, nr=1)
            for q2 in range(NQ2):
                vcopy("vector", FAR.v(KARR + q2 * 512, 512, nr=1), s1[q2])
                vcopy("vector", FAR.v(KARR + 1024 + q2 * 512, 512, nr=1), s2[q2])
            tA = FAR.v(VARR, tok, nr=1)
            tB = FAR.v(VARR + 1024, tok, nr=1)
            tss("vector", mean_r, r_s1, 1.0 / d, ALU.mult)
            tss("vector", tA, r_s2, 1.0 / d, ALU.mult)
            tt("vector", tB, mean_r, mean_r, ALU.mult)
            tt("vector", tA, tA, tB, ALU.subtract)
            tss("vector", tA, tA, EPS, ALU.add)
            recip(tB, tA)
            act(rstd_r, tB, AF.Sqrt)

        def ln_qt(q, mean_r, rstd_r, oh, ol):
            pmb, prb = bank(0, qt), bank(1, qt)
            mm(pmb, onesr.v(0, 128), View(mean_r.ap[0:1, q * qt:(q + 1) * qt],
                                        mean_r.bufs), True, True)
            mm(prb, onesr.v(0, 128), View(rstd_r.ap[0:1, q * qt:(q + 1) * qt],
                                        rstd_r.bufs), True, True)
            for kc in range(KC):
                xv = xT.v(kc * tok + q * qt, qt)
                t1 = FAR.v(SCA + (kc % 2) * 512, qt)
                t2 = FAR.v(SCB + (kc % 2) * 512, qt)
                tt("vector", t1, xv, pmb, ALU.subtract)
                tt("vector", t2, t1, prb, ALU.mult)
                split_pair(t2, BAR.v(oh + kc * qt, qt), BAR.v(ol + kc * qt, qt))

        def proj3p(dram_ap, movh_fn, movl_fn, out_fn, OCn, kcn, N):
            pk = max(1, 1024 // (OCG * 128))
            npieces = (kcn + pk - 1) // pk
            ogn = OCn // OCG
            for og in range(ogn):
                base = og * kcn * OCG * 128
                for pc in range(npieces):
                    k0 = pc * pk
                    k1 = min(kcn, k0 + pk)
                    cols = (k1 - k0) * OCG * 128
                    arrv = FAR.v(ARR + (pc % 3) * 1024, cols)
                    dma(arrv, dram_ap[:, base + k0 * OCG * 128:
                                      base + k1 * OCG * 128])
                    wh = BAR.v(WPH + k0 * OCG * 128, cols)
                    wl = BAR.v(WPL + k0 * OCG * 128, cols)
                    split_pair(arrv, wh, wl, hi_eng="scalar", lo_eng="vector")
                    for kc in range(k0, k1):
                        for oi in range(OCG):
                            sh = BAR.v(WPH + (kc * OCG + oi) * 128, 128)
                            sl = BAR.v(WPL + (kc * OCG + oi) * 128, 128)
                            mm3(bank((og % 2) * OCG + oi, N), sh, sl,
                                movh_fn(kc), movl_fn(kc),
                                kc == 0, kc == kcn - 1)
                for oi in range(OCG):
                    out_fn(og * OCG + oi, bank((og % 2) * OCG + oi, N))

        mean_r = rows.v(0, tok)
        rstd_r = rows.v(1024, tok)
        ln_stats(mean_r, rstd_r)

        for q in range(NQ):
            ln_qt(q, mean_r, rstd_r, LNH, LNL)

            def q_out(oc, psv):
                split_pair(psv, BAR.v(QH + oc * qt, qt), BAR.v(QL + oc * qt, qt))
            proj3p(dr["wqT"],
                   lambda kc: BAR.v(LNH + kc * qt, qt),
                   lambda kc: BAR.v(LNL + kc * qt, qt),
                   q_out, KC, KC, qt)

            NQD = KVC // QUAD
            pqk2 = [PS.v(0, QUAD * qt), PS.v(QUAD * qt, QUAD * qt)]
            pav = bank(4, qt)
            pr = PS.v(2560, 2 * qt, nr=1)
            prb = bank(0, qt)
            tail_fns = []
            for h in range(heads):
                def kv_dma(kq, h=h):
                    ka = FAR.v(KARR + (kq % 2) * KQB, KQB)
                    va = FAR.v(VARR + (kq % 2) * KQB, KQB)
                    dma(ka, dr["kT"][:, h * kv + kq * KQB:
                                     h * kv + (kq + 1) * KQB])
                    dma(va, dr["vv"][:, h * kv + kq * KQB:
                                     h * kv + (kq + 1) * KQB])

                def kv_split(kq, h=h):
                    s = (kq % 2) * KQB
                    ka = FAR.v(KARR + (kq % 2) * KQB, KQB)
                    va = FAR.v(VARR + (kq % 2) * KQB, KQB)
                    split_pair(ka, BAR.v(KHQ + s, KQB), BAR.v(KLQ + s, KQB),
                               hi_eng="scalar", lo_eng="gpsimd")
                    split_pair(va, BAR.v(VHQ + s, KQB), BAR.v(VLQ + s, KQB),
                               hi_eng="scalar", lo_eng="gpsimd")
                if h == 0:
                    kv_dma(0)
                    kv_dma(1)
                    kv_split(0)
                qh = BAR.v(QH + h * qt, qt)
                ql = BAR.v(QL + h * qt, qt)
                pend = []

                def av_mms(kq, h=h):
                    s_p = (kq % NPSLOT) * 2 * QUAD * qt
                    s = (kq % 2) * KQB
                    for kc in range(kq * QUAD, (kq + 1) * QUAD):
                        off = (kc % QUAD) * 128
                        vh = BAR.v(VHQ + s + off, 128)
                        vl = BAR.v(VLQ + s + off, 128)
                        phv = BAR.v(PB + s_p + (kc % QUAD) * qt, qt)
                        plv = BAR.v(PB + s_p + QUAD * qt + (kc % QUAD) * qt, qt)
                        mm3(pav, vh, vl, phv, plv, kc == 0, kc == KVC - 1)
                    for j in range(2 * QUAD * qt // 512):
                        mm(pr, onescb.v(0, 1),
                           BAR.v(PB + s_p + j * 512, 512),
                           kq == 0 and j == 0,
                           kq == NQD - 1 and j == 2 * QUAD * qt // 512 - 1)

                for kq in range(NQD):
                    if pend:
                        av_mms(pend.pop(0))
                    if kq == 1 and tail_fns:
                        tail_fns.pop(0)()
                    pqk = pqk2[kq % 2]
                    s = (kq % 2) * KQB
                    if kq + 1 < NQD:
                        kv_split(kq + 1)
                    elif h + 1 < heads:
                        pass
                    if kq + 2 < NQD:
                        kv_dma(kq + 2)
                    elif h + 1 < heads and kq + 2 - NQD < 2:
                        kv_dma(kq + 2 - NQD, h=h + 1)
                    for kc in range(kq * QUAD, (kq + 1) * QUAD):
                        off = (kc % QUAD) * 128
                        kh = BAR.v(KHQ + s + off, 128)
                        kl = BAR.v(KLQ + s + off, 128)
                        pq = View(pqk.ap[:, (kc % QUAD) * qt:
                                         (kc % QUAD) * qt + qt], pqk.bufs)
                        mm3(pq, kh, kl, qh, ql, (kc % 2) == 0, (kc % 2) == 1)
                    pf = FAR.v(SCA + (kq % 2) * 1024, QUAD * qt)
                    act(pf, pqk, AF.Exp, scale=SCALE)
                    s_p = (kq % NPSLOT) * 2 * QUAD * qt
                    phv = BAR.v(PB + s_p, QUAD * qt)
                    plv = BAR.v(PB + s_p + QUAD * qt, QUAD * qt)
                    vcopy("vector", phv, pf)
                    tt("vector", plv, pf, phv, ALU.subtract)
                    pend.append(kq)
                for pp in pend:
                    av_mms(pp)
                if h + 1 < heads:
                    kv_split(0, h=h + 1)
                rsum = rows.v(2048, 2 * qt)
                vcopy("vector", rsum, pr)
                rec = rows.v(2048 + 2 * qt, qt)
                tt("vector", rec, rows.v(2048, qt), rows.v(2048 + qt, qt),
                   ALU.add)
                recip(rec, rec)

                def tail(h=h, rec=rec):
                    mm(prb, onesr.v(0, 128), rec, True, True)
                    rbf = FAR.v(RBF, qt)
                    vcopy("vector", rbf, prb)
                    t1 = FAR.v(SCC, qt)
                    tt("vector", t1, pav, rbf, ALU.mult)
                    split_pair(t1, BAR.v(AOH + h * qt, qt),
                               BAR.v(AOL + h * qt, qt))
                tail_fns.append(tail)
                tail_fns.pop(0)()
                if (isinstance(stop_after, tuple) and stop_after[0] == "head"
                        and q == stop_after[1] and h == stop_after[2]):
                    while tail_fns:
                        tail_fns.pop(0)()
                    kb.generate()
                    return nc
            while tail_fns:
                tail_fns.pop(0)()

            def wo_out(oc, psv, q=q):
                xv = xT.v(oc * tok + q * qt, qt)
                tt("vector", xv, psv, xv, ALU.add)
            proj3p(dr["woT"],
                   lambda kc: BAR.v(AOH + kc * qt, qt),
                   lambda kc: BAR.v(AOL + kc * qt, qt),
                   wo_out, KC, KC, qt)

        if stop_after == "attn":
            outs = []
            XS = min(4096, KC * tok)
            for c in range(KC * tok // XS):
                outs.append(dma_out(dr["yT"][:, c * XS:(c + 1) * XS],
                                    xT.v(c * XS, XS)))
            kb.op("gpsimd", None, dep=outs)
            kb.generate()
            return nc

        ln_stats(mean_r, rstd_r)
        pmb2 = [bank(i, 512) for i in range(NQ2)]
        prb2 = [bank(NQ2 + i, 512) for i in range(NQ2)]
        plog = [bank(2 * NQ2 + i, 512, nr=16) for i in range(NQ2)]
        for q2 in range(NQ2):
            mm(pmb2[q2], onesr.v(0, 128), rows.v(q2 * 512, 512), True, True)
            mm(prb2[q2], onesr.v(0, 128), rows.v(1024 + q2 * 512, 512), True, True)
        for kc in range(KC):
            lnf0 = SCA if kc % 2 == 0 else SCB
            for q2 in range(NQ2):
                xv = xT.v(kc * tok + q2 * 512, 512)
                t1 = FAR.v(RBF, 512)
                lvv = FAR.v(lnf0 + q2 * 512, 512)
                tt("vector", t1, xv, pmb2[q2], ALU.subtract)
                tt("vector", lvv, t1, prb2[q2], ALU.mult)
                mm(plog[q2], rwTs.v(kc * nexp, nexp), lvv, kc == 0, kc == KC - 1)
            lnfull = FAR.v(lnf0, tok)
            pch = BAR.v(PCH, tok)
            pcl = BAR.v(PCL, tok)
            split_pair(lnfull, pch, pcl)
            for tc in range(TC):
                for (srcoff, dst) in ((PCH, TMH), (PCL, TML)):
                    ptv = PT.v((tc % 2) * 1024, 128)
                    transpose(ptv, BAR.v(srcoff + tc * 128, 128), identb.v(0, 128))
                    vcopy("vector", BAR.v(dst + tc * d + kc * 128, 128), ptv)
        sp1 = dma_out(dr["tms"][:, 0:TC * d], BAR.v(TMH, TC * d), eng="sync")
        sp2 = dma_out(dr["tms"][:, TC * d:2 * TC * d], BAR.v(TML, TC * d),
                      eng="sync")

        for q2 in range(NQ2):
            vcopy("vector", FAR.v(KARR + q2 * 512, 512, nr=16), plog[q2])
        for tc in range(TC):
            pt = bank(tc % 2, nexp)
            transpose(pt, FAR.v(KARR + tc * 128, 128, nr=16),
                      identf.v(0, 16, nr=16))
            vcopy("vector", logtm.v(tc * nexp, nexp), pt)
        cA, cB, cC, cD = (cols4.v(i, 1) for i in range(4))
        for tc in range(TC):
            lg = logtm.v(tc * nexp, nexp)
            sel = smsk.v(0, nexp)
            selm = smsk.v(nexp, nexp)
            msk1 = smsk.v(2 * nexp, nexp)
            msk2 = smsk.v(3 * nexp, nexp)
            ecol = smsk.v(4 * nexp, nexp)
            tt("vector", sel, lg, biasbc.v(0, nexp), ALU.add)
            red(cA, sel, ALU.max)
            tss("vector", msk1, sel, cA, ALU.is_equal)
            stt("vector", selm, msk1, -1e30, sel, ALU.mult, ALU.add)
            red(cB, selm, ALU.max)
            tss("vector", msk2, selm, cB, ALU.is_equal)
            red(cC, lg, ALU.max)
            tss("vector", cC, cC, -1.0, ALU.mult)
            act(ecol, lg, AF.Exp, bias=cC)
            tt("vector", sel, ecol, msk1, ALU.mult)
            red(cA, sel, ALU.add)
            tt("vector", sel, ecol, msk2, ALU.mult)
            red(cB, sel, ALU.add)
            tt("vector", cD, cA, cB, ALU.add)
            recip(cD, cD)
            tt("vector", cA, cA, cD, ALU.mult)
            tt("vector", cB, cB, cD, ALU.mult)
            cmbv = cmbs.v(tc * nexp, nexp)
            tss("vector", sel, msk2, cB, ALU.mult)
            stt("vector", cmbv, msk1, cA, sel, ALU.mult, ALU.add)
            split_pair(cmbv, cmbh.v(tc * nexp, nexp), cmbl.v(tc * nexp, nexp))
            tt("vector", m12.v(tc * nexp, nexp), msk1, msk2, ALU.add)
        for tc in range(TC):
            pc = bank(2 + tc % 2, nexp)
            for j in range(tc + 1):
                statv = tri2.v(0, 128) if j == tc else ones2d.v(0, 128)
                mm(pc, statv, m12.v(j * nexp, nexp), j == 0, j == tc)
            vcopy("vector", cumcs.v(tc * nexp, nexp), pc)
            tt("vector", slotsS.v(tc * nexp, nexp), m12.v(tc * nexp, nexp),
               cumcs.v(tc * nexp, nexp), ALU.mult)
            tss("vector", slotsS.v(tc * nexp, nexp), slotsS.v(tc * nexp, nexp),
                -1.0, ALU.add)

        for q in range(NQ):
            ln_qt(q, mean_r, rstd_r, FMH, FML)
            for j in range(SHC):
                psu, psg = bank(0, qt), bank(1, qt)
                for pc in range(4):
                    cols = KC * 256 // 4
                    arrv = FAR.v(ARR + (pc % 3) * 1024, cols)
                    dma(arrv, dr["susg"][:, j * KC * 256 + pc * cols:
                                         j * KC * 256 + (pc + 1) * cols])
                    split_pair(arrv, BAR.v(SGH + pc * cols, cols),
                               BAR.v(SGL + pc * cols, cols),
                               hi_eng="scalar", lo_eng="vector")
                for kc in range(KC):
                    mh = BAR.v(FMH + kc * qt, qt)
                    ml = BAR.v(FML + kc * qt, qt)
                    mm3(psu, BAR.v(SGH + kc * 256, 128),
                        BAR.v(SGL + kc * 256, 128), mh, ml, kc == 0, kc == KC - 1)
                    mm3(psg, BAR.v(SGH + kc * 256 + 128, 128),
                        BAR.v(SGL + kc * 256 + 128, 128), mh, ml,
                        kc == 0, kc == KC - 1)
                t1 = FAR.v(SCA, qt)
                t2 = FAR.v(SCA + 512, qt)
                act(t1, psu, AF.Silu)
                tt("vector", t2, t1, psg, ALU.mult)
                split_pair(t2, BAR.v(HSH + j * qt, qt), BAR.v(HSL + j * qt, qt))

            def sd_out(oc, psv, q=q):
                xv = xT.v(oc * tok + q * qt, qt)
                tt("vector", xv, psv, xv, ALU.add)
            proj3p(dr["sdT"],
                   lambda kc: BAR.v(HSH + kc * qt, qt),
                   lambda kc: BAR.v(HSL + kc * qt, qt),
                   sd_out, KC, SHC, qt)

        rl1 = dma(BAR.v(TMH, TC * d), dr["tms"][:, 0:TC * d], eng="sync",
                  dep=[sp1, sp2])
        rl2 = dma(BAR.v(TML, TC * d), dr["tms"][:, TC * d:2 * TC * d],
                  eng="sync", dep=[sp1, sp2])

        for e in range(nexp):
            for tc in range(TC):
                tss("vector", sbfS.v(tc * cap, cap), iotf.v(0, cap),
                    View(slotsS.t_ap[:, tc * nexp + e:tc * nexp + e + 1],
                         slotsS.v(tc * nexp, nexp).bufs), ALU.is_equal)
            for ci, (c0, cw) in enumerate(CCS):
                for tc in range(TC):
                    ptv = PT.v((tc % 2) * 1024, 128, nr=cw)
                    transpose(ptv, View(sbfS.t_ap[:, tc * cap + c0:
                                                  tc * cap + c0 + cw],
                                        sbfS.v(tc * cap, cap).bufs),
                              identb.v(0, 128))
                    vcopy("vector", stm.v(ci * tok + tc * 128, 128, nr=cw), ptv)
                pcc = bank(0 + ci, 1, nr=cw)
                for tc in range(TC):
                    sv = View(sbfS.t_ap[:, tc * cap + c0:tc * cap + c0 + cw],
                              sbfS.v(tc * cap, cap).bufs)
                    ch = View(cmbh.t_ap[:, tc * nexp + e:tc * nexp + e + 1],
                              cmbh.v(tc * nexp, nexp).bufs)
                    cl = View(cmbl.t_ap[:, tc * nexp + e:tc * nexp + e + 1],
                              cmbl.v(tc * nexp, nexp).bufs)
                    mm(pcc, sv, ch, tc == 0, False)
                    mm(pcc, sv, cl, False, tc == TC - 1)
                vcopy("vector", cmbcol.v(ci, 1, nr=cw), pcc)
            nft = d // 512
            for kcg in range(0, KC, 4):
                for kc in range(kcg, min(kcg + 4, KC)):
                    pg = bank(2 + kc - kcg, cap)
                    sv = sbfS
                    for tc in range(TC):
                        th = BAR.v(TMH + tc * d + kc * 128, 128)
                        tl = BAR.v(TML + tc * d + kc * 128, 128)
                        smov = sbfS.v(tc * cap, cap)
                        mm(pg, th, smov, tc == 0, False)
                        mm(pg, tl, smov, False, tc == TC - 1)
                    vcopy("vector", FAR.v(FGT + kc * cap, cap), pg)
            for wname in ("wu", "wg"):
                dramw = dr[wname]
                ngrp = (EFC + 3) // 4
                for g2 in range(ngrp):
                    gw = min(4, EFC - g2 * 4)
                    base = g2 * KC * 512
                    for pc in range(KC // 2):
                        cols = 2 * gw * 128
                        arrv = FAR.v(ARR + (pc % 3) * 1024, cols)
                        dma(arrv, dramw[e, :, base + pc * 2 * gw * 128:
                                        base + (pc + 1) * 2 * gw * 128])
                        for k in range(2):
                            kc = pc * 2 + k
                            for oi in range(gw):
                                sv = FAR.v(ARR + (pc % 3) * 1024 +
                                           (k * gw + oi) * 128, 128)
                                mm(bank(2 + oi, cap), sv, FAR.v(FGT + kc * cap, cap),
                                   kc == 0, kc == KC - 1)
                    for oi in range(gw):
                        efc = g2 * 4 + oi
                        if wname == "wu":
                            vcopy("vector", FAR.v(FU + efc * cap, cap),
                                  bank(2 + oi, cap))
                        else:
                            t1 = FAR.v(SCA, cap)
                            act(t1, FAR.v(FU + efc * cap, cap), AF.Silu)
                            tt("vector", FAR.v(FH + efc * cap, cap), t1,
                               bank(2 + oi, cap), ALU.mult)
            for ftg in range(0, nft, 2):
                nfi = min(2, nft - ftg)
                for kcd in range(EFC):
                    arrv = FAR.v(ARR + (kcd % 3) * 1024, nfi * 512)
                    dma(arrv, dr["wd"][e, :, kcd * d + ftg * 512:
                                       kcd * d + (ftg + nfi) * 512])
                    for fi in range(nfi):
                        for ci, (c0, cw) in enumerate(CCS):
                            pd = bank(2 + fi * 2 + ci, 512, nr=cw)
                            sv = FAR.v(FH + kcd * cap + c0, cw)
                            mv = FAR.v(ARR + (kcd % 3) * 1024 + fi * 512, 512)
                            mm(pd, sv, mv, kcd == 0, kcd == EFC - 1)
                for fi in range(nfi):
                    ft = ftg + fi
                    eo0 = EOS + (ft % 2) * 2048
                    for ci, (c0, cw) in enumerate(CCS):
                        pd = bank(2 + fi * 2 + ci, 512, nr=cw)
                        t1 = FAR.v(SCB + (ci % 2) * 512, 512, nr=cw)
                        tss("vector", t1, pd, cmbcol.v(ci, 1, nr=cw), ALU.mult)
                        split_pair(t1, BAR.v(eo0 + ci * 1024, 512, nr=cw),
                                   BAR.v(eo0 + ci * 1024 + 512, 512, nr=cw))
                    for fl in range(4):
                        fc = ft * 4 + fl
                        for q2 in range(NQ2):
                            psc = bank(q2, 512)
                            first = True
                            for ci, (c0, cw) in enumerate(CCS):
                                eoh = BAR.v(eo0 + ci * 1024 + fl * 128, 128, nr=cw)
                                eol = BAR.v(eo0 + ci * 1024 + 512 + fl * 128,
                                            128, nr=cw)
                                smv = stm.v(ci * tok + q2 * 512, 512, nr=cw)
                                mm(psc, eoh, smv, first, False)
                                mm(psc, eol, smv, False, ci == len(CCS) - 1)
                                first = False
                            xv = xT.v(fc * tok + q2 * 512, 512)
                            tt("vector", xv, psc, xv, ALU.add)

        outs = []
        for c in range(KC * tok // XSTEP):
            outs.append(dma_out(dr["yT"][:, c * XSTEP:(c + 1) * XSTEP],
                                xT.v(c * XSTEP, XSTEP)))
        kb.op("gpsimd", None, dep=outs)

        kb.generate()
    return nc


def prepare_maps(inputs, cfg, ncores):
    tok, d, heads = cfg["tok"], cfg["d"], cfg["heads"]
    nexp, shef, ef = cfg["nexp"], cfg["shef"], cfg["ef"]
    KC = d // 128

    x = np.asarray(inputs["L0_attn_query"], dtype=np.float32)
    kv_k = np.asarray(inputs["L0_attn_kv_k"], dtype=np.float32)
    kv_v = np.asarray(inputs["L0_attn_kv_v"], dtype=np.float32)
    Wq = np.asarray(inputs["L0_attn_Wq"], dtype=np.float32)
    Wo = np.asarray(inputs["L0_attn_Wo"], dtype=np.float32)
    rw = np.asarray(inputs["L0_ffn_router_w"], dtype=np.float32)
    rb = np.asarray(inputs["L0_ffn_router_bias"], dtype=np.float32)
    Wu = np.asarray(inputs["L0_ffn_Wu"], dtype=np.float32)
    Wg = np.asarray(inputs["L0_ffn_Wg"], dtype=np.float32)
    Wd = np.asarray(inputs["L0_ffn_Wd"], dtype=np.float32)
    Su = np.asarray(inputs["L0_ffn_Su"], dtype=np.float32)[0]
    Sg = np.asarray(inputs["L0_ffn_Sg"], dtype=np.float32)[0]
    Sd = np.asarray(inputs["L0_ffn_Sd"], dtype=np.float32)[0]

    B, T, D = x.shape
    ntok = B * T
    xf = x.reshape(ntok, D)

    wqT = pack_proj(np.ascontiguousarray(Wq.T), 256)
    woT = pack_proj(np.ascontiguousarray(Wo.T), 256)
    kvt = kv_k.reshape(-1, heads, 128).transpose(1, 2, 0)
    kT = np.ascontiguousarray(np.concatenate(
        [kvt[h] for h in range(heads)], axis=1))
    vvt = kv_v.reshape(-1, heads, 128).transpose(1, 0, 2)
    vv = np.ascontiguousarray(np.concatenate(
        [vvt[h].reshape(-1, 128, 128).transpose(1, 0, 2).reshape(128, -1)
         for h in range(heads)], axis=1))
    rwT = fm(np.ascontiguousarray(rw.T))
    rbr = np.ascontiguousarray(rb.reshape(1, nexp))
    egc = min(4, ef // 128) * 128
    wu_p = np.stack([pack_proj(Wu[e], egc) for e in range(nexp)])
    wg_p = np.stack([pack_proj(Wg[e], egc) for e in range(nexp)])
    wd_p = np.stack([fm(Wd[e]) for e in range(nexp)])
    SHC = shef // 128
    blocks = []
    for j in range(SHC):
        for kc in range(KC):
            blocks.append(np.concatenate(
                [Su[kc * 128:(kc + 1) * 128, j * 128:(j + 1) * 128],
                 Sg[kc * 128:(kc + 1) * 128, j * 128:(j + 1) * 128]], axis=1))
    susg = np.ascontiguousarray(np.concatenate(blocks, axis=1))
    sdT = pack_proj(np.ascontiguousarray(Sd), 256)

    weights = dict(wqT=wqT, woT=woT, kT=kT, vv=vv, rwT=rwT, rb=rbr,
                   wu=wu_p, wg=wg_p, wd=wd_p, susg=susg, sdT=sdT)

    per = ntok // ncores
    in_maps = []
    for c in range(ncores):
        xs = np.ascontiguousarray(xf[c * per:(c + 1) * per].T)
        m = dict(weights)
        m["xT"] = fm(xs)
        in_maps.append(m)
    return in_maps, (B, T, D)


def kernel(**inputs):
    cfg = dict(FULL)
    in_maps, (B, T, D) = prepare_maps(inputs, cfg, NCORES)
    per = B * T // NCORES
    nc = build(cfg)
    res = run_bass_kernel_spmd(nc, in_maps, core_ids=list(range(NCORES)))
    outs = []
    for c in range(NCORES):
        yT = np.asarray(res.results[c]["yT"])
        outs.append(unfm(yT, D, per).T)
    return np.concatenate(outs, axis=0).reshape(B, T, D).astype(np.float32)


# version 17
# speedup vs baseline: 1.3052x; 1.3052x over previous
import contextlib

import numpy as np

import concourse.bass as bass
import concourse.mybir as mybir
from concourse.bass_utils import run_bass_kernel_spmd

F32 = mybir.dt.float32
BF16 = mybir.dt.bfloat16
I32 = mybir.dt.int32
AF = mybir.ActivationFunctionType
ALU = mybir.AluOpType
AX = mybir.AxisListType

NCORES = 8

FULL = dict(tok=1024, d=2048, heads=16, hd=128, kv=4096, nexp=16, ef=1024,
            cap=192, shef=2048, qt=256)

ENGS = ("sync", "scalar", "vector", "tensor", "gpsimd")


class Op:
    __slots__ = ("eng", "fn", "deps", "dma", "needed", "val", "waits", "tag",
                 "slot")

    def __init__(self, eng, fn, dma, tag):
        self.eng = eng
        self.fn = fn
        self.deps = []
        self.dma = dma
        self.needed = False
        self.val = None
        self.waits = []
        self.tag = tag
        self.slot = 0

    @property
    def semkey(self):
        return f"{self.eng}_dma{self.slot}" if self.dma else self.eng


class Buf:
    __slots__ = ("name", "writer", "readers")

    def __init__(self, name):
        self.name = name
        self.writer = None
        self.readers = []


class View:
    __slots__ = ("ap", "bufs")

    def __init__(self, ap, bufs):
        self.ap = ap
        self.bufs = list(bufs)


class Slab:

    def __init__(self, name, ap, blk_cols, ncols, nrows=128):
        self.name = name
        self.t_ap = ap
        self.blk = blk_cols
        self.ncols = ncols
        self.nrows = nrows
        nblk = (ncols + blk_cols - 1) // blk_cols
        self.bufs = [Buf(f"{name}[{i}]") for i in range(nblk)]

    def v(self, c0, nc, r0=0, nr=None):
        nr = self.nrows if nr is None else nr
        assert 0 <= c0 and c0 + nc <= self.ncols, (self.name, c0, nc)
        b0 = c0 // self.blk
        b1 = (c0 + nc - 1) // self.blk
        return View(self.t_ap[r0:r0 + nr, c0:c0 + nc], self.bufs[b0:b1 + 1])


DMA_SLOTS = {"sync": 8, "scalar": 4, "vector": 4, "tensor": 4, "gpsimd": 4}


class KB:
    def __init__(self, nc, es):
        self.nc = nc
        self.es = es
        self.ops = {e: [] for e in ENGS}
        self.sems = {}
        self.dmacnt = {e: 0 for e in ENGS}
        self.lastdma = {}

    def op(self, eng, fn, r=(), w=(), dep=(), dma=False, tag=""):
        o = Op(eng, fn, dma, tag)
        if dma:
            o.slot = self.dmacnt[eng] % DMA_SLOTS[eng]
            self.dmacnt[eng] += 1
            prev = self.lastdma.get(o.semkey)
            if prev is not None:
                dep = list(dep) + [prev]
            self.lastdma[o.semkey] = o
        deps = []
        for vw in r:
            for b in vw.bufs:
                if b.writer is not None:
                    deps.append(b.writer)
        for vw in w:
            for b in vw.bufs:
                if b.writer is not None:
                    deps.append(b.writer)
                deps.extend(b.readers)
        deps.extend(dep)
        seen = set()
        for d in deps:
            if d is o or id(d) in seen:
                continue
            if d.eng == "tensor" and eng == "tensor" and not d.dma and not dma:
                continue
            seen.add(id(d))
            o.deps.append(d)
        for vw in r:
            for b in vw.bufs:
                b.readers.append(o)
        for vw in w:
            for b in vw.bufs:
                b.writer = o
                b.readers = []
        self.ops[eng].append(o)
        return o

    def generate(self):
        for e in ENGS:
            for o in self.ops[e]:
                for d in o.deps:
                    d.needed = True
        counters = {}
        for e in ENGS:
            for o in self.ops[e]:
                if o.needed:
                    k = o.semkey
                    counters[k] = counters.get(k, 0) + (16 if o.dma else 1)
                    o.val = counters[k]
        for k in counters:
            self.sems[k] = self.es.enter_context(self.nc.semaphore("S" + k))
        water = {e: {} for e in ENGS}
        for e in ENGS:
            for o in self.ops[e]:
                need = {}
                for d in o.deps:
                    k = d.semkey
                    if d.val > need.get(k, 0):
                        need[k] = d.val
                for k, v in sorted(need.items()):
                    if v > water[e].get(k, 0):
                        o.waits.append((k, v))
                        water[e][k] = v
        nc = self.nc
        with nc.Block() as blk:
            for engname in ENGS:
                ops = self.ops[engname]

                def runner(eng, ops=ops, self=self):
                    for o in ops:
                        for (k, v) in o.waits:
                            eng.wait_ge(self.sems[k], v)
                        if o.fn is not None:
                            inst = o.fn(eng)
                            if o.needed:
                                inst.then_inc(self.sems[o.semkey],
                                              16 if o.dma else 1)
                getattr(blk, engname)(runner)


def fm(a):
    r, c = a.shape
    assert r % 128 == 0
    return np.ascontiguousarray(
        a.reshape(r // 128, 128, c).transpose(1, 0, 2).reshape(128, -1))


def unfm(a, r, c):
    return np.ascontiguousarray(
        a.reshape(128, r // 128, c).transpose(1, 0, 2).reshape(r, c))


def pack_proj(wT, ocg_cols):
    k, o = wT.shape
    kc = k // 128
    ogn = o // ocg_cols
    blocks = []
    for og in range(ogn):
        for r in range(kc):
            blocks.append(wT[r * 128:(r + 1) * 128,
                             og * ocg_cols:(og + 1) * ocg_cols])
    return np.ascontiguousarray(np.concatenate(blocks, axis=1))


def build(cfg, stop_after=None):
    tok, d, heads = cfg["tok"], cfg["d"], cfg["heads"]
    kv, nexp, ef, cap = cfg["kv"], cfg["nexp"], cfg["ef"], cfg["cap"]
    shef, qt = cfg["shef"], cfg["qt"]
    KC = d // 128
    TC = tok // 128
    NQ = tok // qt
    KVC = kv // 128
    EFC = ef // 128
    SHC = shef // 128
    NQ2 = tok // 512
    KQ = kv // 4
    KQC = KQ // 128
    CAPM = max(cap, 128)
    SCALE = float(np.float32(1.0 / np.sqrt(np.float64(cfg["hd"]))))
    EPS = 1e-5
    CCS = [(0, min(128, cap))] + ([(128, cap - 128)] if cap > 128 else [])
    OCG = 2

    nc = bass.Bass("TRN2", target_bir_lowering=False, debug=False)

    dr = {}
    dr["xT"] = nc.dram_tensor("xT", [128, KC * tok], F32, kind="ExternalInput").ap()
    dr["wqT"] = nc.dram_tensor("wqT", [128, KC * d], F32, kind="ExternalInput").ap()
    dr["woT"] = nc.dram_tensor("woT", [128, KC * d], F32, kind="ExternalInput").ap()
    dr["kT"] = nc.dram_tensor("kT", [128, heads * kv], F32, kind="ExternalInput").ap()
    dr["vv"] = nc.dram_tensor("vv", [128, heads * kv], F32, kind="ExternalInput").ap()
    dr["rwT"] = nc.dram_tensor("rwT", [128, KC * nexp], F32, kind="ExternalInput").ap()
    dr["rb"] = nc.dram_tensor("rb", [1, nexp], F32, kind="ExternalInput").ap()
    dr["wu"] = nc.dram_tensor("wu", [nexp, 128, KC * ef], F32, kind="ExternalInput").ap()
    dr["wg"] = nc.dram_tensor("wg", [nexp, 128, KC * ef], F32, kind="ExternalInput").ap()
    dr["wd"] = nc.dram_tensor("wd", [nexp, 128, EFC * d], F32, kind="ExternalInput").ap()
    dr["susg"] = nc.dram_tensor("susg", [128, SHC * KC * 256], F32, kind="ExternalInput").ap()
    dr["sdT"] = nc.dram_tensor("sdT", [128, SHC * d], F32, kind="ExternalInput").ap()
    dr["yT"] = nc.dram_tensor("yT", [128, KC * tok], F32, kind="ExternalOutput").ap()
    dr["tms"] = nc.dram_tensor("tms", [128, 2 * TC * d], BF16, kind="Internal").ap()

    with contextlib.ExitStack() as es:
        E = es.enter_context
        kb = KB(nc, es)

        def sb(name, shape, dt):
            return E(nc.sbuf_tensor(name, shape, dt)).ap()

        xT = Slab("xT", sb("s_xT", [128, KC * tok], F32), 512, KC * tok)
        FARN = 11264
        FAR = Slab("FAR", sb("s_FAR", [128, FARN], F32), 512, FARN)

        A = KC * qt
        WPN = KC * OCG * 128
        QUAD = min(4, KVC)
        KQB = QUAD * 128
        NPSLOT = 3
        XR = max(2 * WPN, 12 * KQB + NPSLOT * 2 * QUAD * qt)
        attn_need = 6 * A + XR
        ln2_need = 2 * TC * d + 2 * tok
        shared_need = 2 * A + 2 * SHC * qt + 2 * KC * 256 + 2 * WPN
        exp_need = 2 * TC * d + 2 * 4 * 512
        BARN = max(attn_need, ln2_need, shared_need, exp_need)
        BAR = Slab("BAR", sb("s_BAR", [128, BARN], BF16), 512, BARN)

        LNH, LNL, QH, QL, AOH, AOL = (i * A for i in range(6))
        WPH, WPL = 6 * A, 6 * A + WPN
        KHQ, KLQ, VHQ, VLQ = (6 * A + i * 3 * KQB for i in range(4))
        PB = 6 * A + 12 * KQB
        TMH, TML, PCH, PCL = 0, TC * d, 2 * TC * d, 2 * TC * d + tok
        FMH, FML = 0, A
        HSH, HSL = 2 * A, 2 * A + SHC * qt
        SGH, SGL = 2 * A + 2 * SHC * qt, 2 * A + 2 * SHC * qt + KC * 256
        EOS = 2 * TC * d

        ioti = Slab("ioti", sb("s_ioti", [128, CAPM], I32), CAPM, CAPM)
        iotci = Slab("iotci", sb("s_iotci", [128, 1], I32), 1, 1)
        iotf = Slab("iotf", sb("s_iotf", [128, CAPM], F32), CAPM, CAPM)
        iotc = Slab("iotc", sb("s_iotc", [128, 1], F32), 1, 1)
        identf = Slab("identf", sb("s_identf", [128, 128], F32), 128, 128)
        identb = Slab("identb", sb("s_identb", [128, 128], BF16), 128, 128)
        tri2 = Slab("tri2", sb("s_tri2", [128, 128], F32), 128, 128)
        ones2d = Slab("ones2d", sb("s_ones2d", [128, 128], F32), 128, 128)
        onesc = Slab("onesc", sb("s_onesc", [128, 1], F32), 1, 1)
        onescb = Slab("onescb", sb("s_onescb", [128, 1], BF16), 1, 1)
        onesr = Slab("onesr", sb("s_onesr", [1, 128], F32), 128, 128, nrows=1)

        rows = Slab("rows", sb("s_rows", [1, 3072], F32), 512, 3072, nrows=1)
        cols4 = Slab("cols4", sb("s_cols4", [128, 4], F32), 1, 4)
        rwTs = Slab("rwTs", sb("s_rwTs", [128, KC * nexp], F32), 512, KC * nexp)
        rbrow = Slab("rbrow", sb("s_rbrow", [1, nexp], F32), nexp, nexp, nrows=1)
        biasbc = Slab("biasbc", sb("s_biasbc", [128, nexp], F32), nexp, nexp)
        logtm = Slab("logtm", sb("s_logtm", [128, TC * nexp], F32), nexp, TC * nexp)
        m12 = Slab("m12", sb("s_m12", [128, TC * nexp], F32), nexp, TC * nexp)
        cmbs = Slab("cmbs", sb("s_cmbs", [128, TC * nexp], F32), nexp, TC * nexp)
        cmbh = Slab("cmbh", sb("s_cmbh", [128, TC * nexp], BF16), nexp, TC * nexp)
        cmbl = Slab("cmbl", sb("s_cmbl", [128, TC * nexp], BF16), nexp, TC * nexp)
        cumcs = Slab("cumcs", sb("s_cumcs", [128, TC * nexp], F32), nexp, TC * nexp)
        slotsS = Slab("slots", sb("s_slots", [128, TC * nexp], F32), nexp, TC * nexp)
        smsk = Slab("smsk", sb("s_smsk", [128, 5 * nexp], F32), nexp, 5 * nexp)
        sbfS = Slab("sbf", sb("s_sbf", [128, TC * cap], BF16), cap, TC * cap)
        stm = Slab("stm", sb("s_stm", [128, len(CCS) * tok], BF16), 512, len(CCS) * tok)
        cmbcol = Slab("cmbcol", sb("s_cmbcol", [128, len(CCS)], F32), 1, len(CCS))

        PS = Slab("PS", E(nc.psum_tensor("p_PS", [128, 3072], F32)).ap(), 512, 3072)
        PT = Slab("PT", E(nc.psum_tensor("p_PT", [128, 2048], BF16)).ap(), 1024, 2048)

        def bank(b, cols=512, r0=0, nr=128):
            return PS.v(b * 512, cols, r0, nr)

        SCA, SCB = 0, 1024
        PF = [2048, 2560]
        RBF, SCC = 3072, 3584
        KARR, VARR, ARR = 4096, 6144, 8192
        FH, FU = 2048, 2048 + EFC * cap
        FGT = FU + EFC * cap
        assert FGT + KC * cap <= ARR, (FGT + KC * cap, ARR)
        assert ARR + 3 * 1024 <= FARN

        def mm(psv, statv, movv, start, stop, dep=()):
            return kb.op("tensor", lambda T, o=psv.ap, l=statv.ap, r=movv.ap,
                         a=start, b=stop: T.matmul(o, l, r, start=a, stop=b),
                         r=[statv, movv], w=[psv], dep=dep, tag="mm")

        def mm3(psv, sh, sl, mh, ml, start, stop):
            mm(psv, sh, mh, start, False)
            mm(psv, sh, ml, False, False)
            mm(psv, sl, mh, False, stop)

        def vcopy(eng, dst, src, dep=()):
            kb.op(eng, lambda Eo, o=dst.ap, i=src.ap: Eo.tensor_copy(out=o, in_=i),
                  r=[src], w=[dst], dep=dep, tag="copy")

        def scopy(dst, src, dep=()):
            kb.op("scalar", lambda Eo, o=dst.ap, i=src.ap: Eo.copy(o, i),
                  r=[src], w=[dst], dep=dep, tag="scopy")

        def tt(eng, dst, a, b, opr, dep=()):
            kb.op(eng, lambda Eo, o=dst.ap, x=a.ap, y=b.ap, p=opr:
                  Eo.tensor_tensor(out=o, in0=x, in1=y, op=p),
                  r=[a, b], w=[dst], dep=dep, tag="tt")

        def tss(eng, dst, a, scal, opr, dep=()):
            if isinstance(scal, View):
                kb.op(eng, lambda Eo, o=dst.ap, x=a.ap, s=scal.ap, p=opr:
                      Eo.tensor_single_scalar(out=o, in_=x, scalar=s, op=p),
                      r=[a, scal], w=[dst], dep=dep, tag="tss")
            else:
                kb.op(eng, lambda Eo, o=dst.ap, x=a.ap, s=scal, p=opr:
                      Eo.tensor_single_scalar(out=o, in_=x, scalar=s, op=p),
                      r=[a], w=[dst], dep=dep, tag="tss")

        def stt(eng, dst, in0, scal, in1, op0, op1, dep=()):
            sc = scal.ap if isinstance(scal, View) else scal
            rr = [in0, in1] + ([scal] if isinstance(scal, View) else [])
            kb.op(eng, lambda Eo, o=dst.ap, x=in0.ap, s=sc, y=in1.ap,
                  p0=op0, p1=op1:
                  Eo.scalar_tensor_tensor(out=o, in0=x, scalar=s, in1=y,
                                          op0=p0, op1=p1),
                  r=rr, w=[dst], dep=dep, tag="stt")

        def act(dst, src, func, scale=1.0, bias=0.0, dep=()):
            bap = bias.ap if isinstance(bias, View) else bias
            rr = [src] + ([bias] if isinstance(bias, View) else [])
            kb.op("scalar", lambda Eo, o=dst.ap, i=src.ap, f=func, s=scale,
                  b=bap: Eo.activation(o, i, f, bias=b, scale=s),
                  r=rr, w=[dst], dep=dep, tag="act")

        def red(dst, src, opr, dep=()):
            kb.op("vector", lambda Eo, o=dst.ap, i=src.ap, p=opr:
                  Eo.tensor_reduce(out=o, in_=i, axis=AX.X, op=p),
                  r=[src], w=[dst], dep=dep, tag="red")

        def recip(dst, src, dep=()):
            kb.op("vector", lambda Eo, o=dst.ap, i=src.ap:
                  Eo.reciprocal(out=o, in_=i), r=[src], w=[dst], dep=dep,
                  tag="recip")

        def memset(eng, dst, val, dep=()):
            kb.op(eng, lambda Eo, o=dst.ap, v=val: Eo.memset(o, v),
                  r=[], w=[dst], dep=dep, tag="memset")

        def dma(dst, src_ap, eng="sync", dep=()):
            return kb.op(eng, lambda Eo, o=dst.ap, i=src_ap:
                         Eo.dma_start(out=o, in_=i),
                         r=[], w=[dst], dep=dep, dma=True, tag="dma")

        def dma_out(dst_ap, src, eng="scalar", dep=()):
            return kb.op(eng, lambda Eo, o=dst_ap, i=src.ap:
                         Eo.dma_start(out=o, in_=i),
                         r=[src], w=[], dep=dep, dma=True, tag="dmao")

        def transpose(psv, src, ident, dep=()):
            kb.op("tensor", lambda T, o=psv.ap, i=src.ap, dd=ident.ap:
                  T.transpose(o, i, dd), r=[src, ident], w=[psv], dep=dep,
                  tag="tr")

        def split_pair(src, dsth, dstl, hi_eng="scalar", lo_eng="vector"):
            if hi_eng == "scalar":
                scopy(dsth, src)
            elif hi_eng == "dma":
                kb.op("gpsimd", lambda Eo, o=dsth.ap, i=src.ap:
                      Eo.dma_start(out=o, in_=i),
                      r=[src], w=[dsth], dma=True, tag="cdma")
            else:
                vcopy(hi_eng, dsth, src)
            tt(lo_eng, dstl, src, dsth, ALU.subtract)

        memset("vector", onesc.v(0, 1), 1.0)
        memset("vector", onescb.v(0, 1), 1.0)
        memset("vector", onesr.v(0, 128), 1.0)
        memset("vector", ones2d.v(0, 128), 1.0)
        kb.op("gpsimd", lambda G, o=iotci.v(0, 1).ap:
              G.iota(o, pattern=[[0, 1]], channel_multiplier=1),
              w=[iotci.v(0, 1)], tag="iota")
        kb.op("gpsimd", lambda G, o=ioti.v(0, CAPM).ap:
              G.iota(o, pattern=[[1, CAPM]], channel_multiplier=0),
              w=[ioti.v(0, CAPM)], tag="iota")
        vcopy("vector", iotc.v(0, 1), iotci.v(0, 1))
        vcopy("vector", iotf.v(0, CAPM), ioti.v(0, CAPM))
        tss("vector", identf.v(0, 128), iotf.v(0, 128), iotc.v(0, 1), ALU.is_equal)
        tss("vector", identb.v(0, 128), iotf.v(0, 128), iotc.v(0, 1), ALU.is_equal)
        tss("vector", tri2.v(0, 128), iotf.v(0, 128), iotc.v(0, 1), ALU.is_ge)
        dma(rwTs.v(0, KC * nexp), dr["rwT"][:, :])
        dma(rbrow.v(0, nexp), dr["rb"][:, :])
        mm(bank(0, nexp), onesr.v(0, 128), rbrow.v(0, nexp), True, True)
        vcopy("vector", biasbc.v(0, nexp), bank(0, nexp))
        XSTEP = min(4096, KC * tok)
        for c in range(KC * tok // XSTEP):
            dma(xT.v(c * XSTEP, XSTEP), dr["xT"][:, c * XSTEP:(c + 1) * XSTEP],
                eng="scalar")

        def ln_stats(mean_r, rstd_r):
            s1 = [bank(i, 512, nr=1) for i in range(NQ2)]
            s2 = [bank(NQ2 + i, 512, nr=1) for i in range(NQ2)]
            for kc in range(KC):
                sq0 = SCA if kc % 2 == 0 else SCB
                for q2 in range(NQ2):
                    xv = xT.v(kc * tok + q2 * 512, 512)
                    sqv = FAR.v(sq0 + (q2 % 2) * 512, 512)
                    act(sqv, xv, AF.Square)
                    mm(s1[q2], onesc.v(0, 1), xv, kc == 0, kc == KC - 1)
                    mm(s2[q2], onesc.v(0, 1), sqv, kc == 0, kc == KC - 1)
            r_s1 = FAR.v(KARR, tok, nr=1)
            r_s2 = FAR.v(KARR + 1024, tok, nr=1)
            for q2 in range(NQ2):
                vcopy("vector", FAR.v(KARR + q2 * 512, 512, nr=1), s1[q2])
                vcopy("vector", FAR.v(KARR + 1024 + q2 * 512, 512, nr=1), s2[q2])
            tA = FAR.v(VARR, tok, nr=1)
            tB = FAR.v(VARR + 1024, tok, nr=1)
            tss("vector", mean_r, r_s1, 1.0 / d, ALU.mult)
            tss("vector", tA, r_s2, 1.0 / d, ALU.mult)
            tt("vector", tB, mean_r, mean_r, ALU.mult)
            tt("vector", tA, tA, tB, ALU.subtract)
            tss("vector", tA, tA, EPS, ALU.add)
            recip(tB, tA)
            act(rstd_r, tB, AF.Sqrt)

        def ln_qt(q, mean_r, rstd_r, oh, ol):
            pmb, prb = bank(0, qt), bank(1, qt)
            mm(pmb, onesr.v(0, 128), View(mean_r.ap[0:1, q * qt:(q + 1) * qt],
                                        mean_r.bufs), True, True)
            mm(prb, onesr.v(0, 128), View(rstd_r.ap[0:1, q * qt:(q + 1) * qt],
                                        rstd_r.bufs), True, True)
            for kc in range(KC):
                xv = xT.v(kc * tok + q * qt, qt)
                t1 = FAR.v(SCA + (kc % 2) * 512, qt)
                t2 = FAR.v(SCB + (kc % 2) * 512, qt)
                tt("vector", t1, xv, pmb, ALU.subtract)
                tt("vector", t2, t1, prb, ALU.mult)
                split_pair(t2, BAR.v(oh + kc * qt, qt), BAR.v(ol + kc * qt, qt))

        def proj3p(dram_ap, movh_fn, movl_fn, out_fn, OCn, kcn, N):
            pk = max(1, 1024 // (OCG * 128))
            npieces = (kcn + pk - 1) // pk
            ogn = OCn // OCG
            for og in range(ogn):
                base = og * kcn * OCG * 128
                for pc in range(npieces):
                    k0 = pc * pk
                    k1 = min(kcn, k0 + pk)
                    cols = (k1 - k0) * OCG * 128
                    arrv = FAR.v(ARR + (pc % 3) * 1024, cols)
                    dma(arrv, dram_ap[:, base + k0 * OCG * 128:
                                      base + k1 * OCG * 128])
                    wh = BAR.v(WPH + k0 * OCG * 128, cols)
                    wl = BAR.v(WPL + k0 * OCG * 128, cols)
                    split_pair(arrv, wh, wl, hi_eng="scalar", lo_eng="vector")
                    for kc in range(k0, k1):
                        for oi in range(OCG):
                            sh = BAR.v(WPH + (kc * OCG + oi) * 128, 128)
                            sl = BAR.v(WPL + (kc * OCG + oi) * 128, 128)
                            mm3(bank((og % 2) * OCG + oi, N), sh, sl,
                                movh_fn(kc), movl_fn(kc),
                                kc == 0, kc == kcn - 1)
                for oi in range(OCG):
                    out_fn(og * OCG + oi, bank((og % 2) * OCG + oi, N))

        mean_r = rows.v(0, tok)
        rstd_r = rows.v(1024, tok)
        ln_stats(mean_r, rstd_r)

        for q in range(NQ):
            ln_qt(q, mean_r, rstd_r, LNH, LNL)

            def q_out(oc, psv):
                split_pair(psv, BAR.v(QH + oc * qt, qt), BAR.v(QL + oc * qt, qt))
            proj3p(dr["wqT"],
                   lambda kc: BAR.v(LNH + kc * qt, qt),
                   lambda kc: BAR.v(LNL + kc * qt, qt),
                   q_out, KC, KC, qt)

            NQD = KVC // QUAD
            pqk2 = [PS.v(0, QUAD * qt), PS.v(QUAD * qt, QUAD * qt)]
            pav = bank(4, qt)
            pr = PS.v(2560, 2 * qt, nr=1)
            prb = bank(0, qt)
            tail_fns = []
            for h in range(heads):
                def kv_dma(kq, h=h):
                    ka = FAR.v(KARR + (kq % 3) * 512, KQB)
                    va = FAR.v(VARR + (kq % 3) * 512, KQB)
                    dma(ka, dr["kT"][:, h * kv + kq * KQB:
                                     h * kv + (kq + 1) * KQB])
                    dma(va, dr["vv"][:, h * kv + kq * KQB:
                                     h * kv + (kq + 1) * KQB])

                def kv_split(kq, h=h):
                    s = (kq % 3) * KQB
                    ka = FAR.v(KARR + (kq % 3) * 512, KQB)
                    va = FAR.v(VARR + (kq % 3) * 512, KQB)
                    split_pair(ka, BAR.v(KHQ + s, KQB), BAR.v(KLQ + s, KQB),
                               hi_eng="scalar", lo_eng="gpsimd")
                    split_pair(va, BAR.v(VHQ + s, KQB), BAR.v(VLQ + s, KQB),
                               hi_eng="scalar", lo_eng="gpsimd")
                if h == 0:
                    kv_dma(0)
                    kv_dma(1)
                    kv_split(0)
                qh = BAR.v(QH + h * qt, qt)
                ql = BAR.v(QL + h * qt, qt)
                pend = []

                def av_mms(kq, h=h):
                    s_p = (kq % NPSLOT) * 2 * QUAD * qt
                    s = (kq % 3) * KQB
                    for kc in range(kq * QUAD, (kq + 1) * QUAD):
                        off = (kc % QUAD) * 128
                        vh = BAR.v(VHQ + s + off, 128)
                        vl = BAR.v(VLQ + s + off, 128)
                        phv = BAR.v(PB + s_p + (kc % QUAD) * qt, qt)
                        plv = BAR.v(PB + s_p + QUAD * qt + (kc % QUAD) * qt, qt)
                        mm3(pav, vh, vl, phv, plv, kc == 0, kc == KVC - 1)
                    for j in range(2 * QUAD * qt // 512):
                        mm(pr, onescb.v(0, 1),
                           BAR.v(PB + s_p + j * 512, 512),
                           kq == 0 and j == 0,
                           kq == NQD - 1 and j == 2 * QUAD * qt // 512 - 1)

                for kq in range(NQD):
                    if len(pend) > 1:
                        av_mms(pend.pop(0))
                    if kq == 1 and tail_fns:
                        tail_fns.pop(0)()
                    pqk = pqk2[kq % 2]
                    s = (kq % 3) * KQB
                    if kq + 1 < NQD:
                        kv_split(kq + 1)
                    if kq + 2 < NQD:
                        kv_dma(kq + 2)
                    elif h + 1 < heads and kq + 2 - NQD < 2:
                        kv_dma(kq + 2 - NQD, h=h + 1)
                    for kc in range(kq * QUAD, (kq + 1) * QUAD):
                        off = (kc % QUAD) * 128
                        kh = BAR.v(KHQ + s + off, 128)
                        kl = BAR.v(KLQ + s + off, 128)
                        pq = View(pqk.ap[:, (kc % QUAD) * qt:
                                         (kc % QUAD) * qt + qt], pqk.bufs)
                        mm3(pq, kh, kl, qh, ql, (kc % 2) == 0, (kc % 2) == 1)
                    pf = FAR.v(SCA + (kq % 2) * 1024, QUAD * qt)
                    act(pf, pqk, AF.Exp, scale=SCALE)
                    s_p = (kq % NPSLOT) * 2 * QUAD * qt
                    phv = BAR.v(PB + s_p, QUAD * qt)
                    plv = BAR.v(PB + s_p + QUAD * qt, QUAD * qt)
                    vcopy("vector", phv, pf)
                    tt("vector", plv, pf, phv, ALU.subtract)
                    pend.append(kq)
                for pp in pend:
                    av_mms(pp)
                if h + 1 < heads:
                    kv_split(0, h=h + 1)
                rsum = rows.v(2048, 2 * qt)
                vcopy("vector", rsum, pr)
                rec = rows.v(2048 + 2 * qt, qt)
                tt("vector", rec, rows.v(2048, qt), rows.v(2048 + qt, qt),
                   ALU.add)
                recip(rec, rec)

                def tail(h=h, rec=rec):
                    mm(prb, onesr.v(0, 128), rec, True, True)
                    rbf = FAR.v(RBF, qt)
                    vcopy("vector", rbf, prb)
                    t1 = FAR.v(SCC, qt)
                    tt("vector", t1, pav, rbf, ALU.mult)
                    split_pair(t1, BAR.v(AOH + h * qt, qt),
                               BAR.v(AOL + h * qt, qt))
                tail_fns.append(tail)
                if (isinstance(stop_after, tuple) and stop_after[0] == "head"
                        and q == stop_after[1] and h == stop_after[2]):
                    while tail_fns:
                        tail_fns.pop(0)()
                    kb.generate()
                    return nc
            while tail_fns:
                tail_fns.pop(0)()

            def wo_out(oc, psv, q=q):
                xv = xT.v(oc * tok + q * qt, qt)
                tt("vector", xv, psv, xv, ALU.add)
            proj3p(dr["woT"],
                   lambda kc: BAR.v(AOH + kc * qt, qt),
                   lambda kc: BAR.v(AOL + kc * qt, qt),
                   wo_out, KC, KC, qt)

        if stop_after == "attn":
            outs = []
            XS = min(4096, KC * tok)
            for c in range(KC * tok // XS):
                outs.append(dma_out(dr["yT"][:, c * XS:(c + 1) * XS],
                                    xT.v(c * XS, XS)))
            kb.op("gpsimd", None, dep=outs)
            kb.generate()
            return nc

        ln_stats(mean_r, rstd_r)
        pmb2 = [bank(i, 512) for i in range(NQ2)]
        prb2 = [bank(NQ2 + i, 512) for i in range(NQ2)]
        plog = [bank(2 * NQ2 + i, 512, nr=16) for i in range(NQ2)]
        for q2 in range(NQ2):
            mm(pmb2[q2], onesr.v(0, 128), rows.v(q2 * 512, 512), True, True)
            mm(prb2[q2], onesr.v(0, 128), rows.v(1024 + q2 * 512, 512), True, True)
        for kc in range(KC):
            lnf0 = SCA if kc % 2 == 0 else SCB
            for q2 in range(NQ2):
                xv = xT.v(kc * tok + q2 * 512, 512)
                t1 = FAR.v(RBF, 512)
                lvv = FAR.v(lnf0 + q2 * 512, 512)
                tt("vector", t1, xv, pmb2[q2], ALU.subtract)
                tt("vector", lvv, t1, prb2[q2], ALU.mult)
                mm(plog[q2], rwTs.v(kc * nexp, nexp), lvv, kc == 0, kc == KC - 1)
            lnfull = FAR.v(lnf0, tok)
            pch = BAR.v(PCH, tok)
            pcl = BAR.v(PCL, tok)
            split_pair(lnfull, pch, pcl)
            for tc in range(TC):
                for (srcoff, dst) in ((PCH, TMH), (PCL, TML)):
                    ptv = PT.v((tc % 2) * 1024, 128)
                    transpose(ptv, BAR.v(srcoff + tc * 128, 128), identb.v(0, 128))
                    vcopy("vector", BAR.v(dst + tc * d + kc * 128, 128), ptv)
        sp1 = dma_out(dr["tms"][:, 0:TC * d], BAR.v(TMH, TC * d), eng="sync")
        sp2 = dma_out(dr["tms"][:, TC * d:2 * TC * d], BAR.v(TML, TC * d),
                      eng="sync")

        for q2 in range(NQ2):
            vcopy("vector", FAR.v(KARR + q2 * 512, 512, nr=16), plog[q2])
        for tc in range(TC):
            pt = bank(tc % 2, nexp)
            transpose(pt, FAR.v(KARR + tc * 128, 128, nr=16),
                      identf.v(0, 16, nr=16))
            vcopy("vector", logtm.v(tc * nexp, nexp), pt)
        cA, cB, cC, cD = (cols4.v(i, 1) for i in range(4))
        for tc in range(TC):
            lg = logtm.v(tc * nexp, nexp)
            sel = smsk.v(0, nexp)
            selm = smsk.v(nexp, nexp)
            msk1 = smsk.v(2 * nexp, nexp)
            msk2 = smsk.v(3 * nexp, nexp)
            ecol = smsk.v(4 * nexp, nexp)
            tt("vector", sel, lg, biasbc.v(0, nexp), ALU.add)
            red(cA, sel, ALU.max)
            tss("vector", msk1, sel, cA, ALU.is_equal)
            stt("vector", selm, msk1, -1e30, sel, ALU.mult, ALU.add)
            red(cB, selm, ALU.max)
            tss("vector", msk2, selm, cB, ALU.is_equal)
            red(cC, lg, ALU.max)
            tss("vector", cC, cC, -1.0, ALU.mult)
            act(ecol, lg, AF.Exp, bias=cC)
            tt("vector", sel, ecol, msk1, ALU.mult)
            red(cA, sel, ALU.add)
            tt("vector", sel, ecol, msk2, ALU.mult)
            red(cB, sel, ALU.add)
            tt("vector", cD, cA, cB, ALU.add)
            recip(cD, cD)
            tt("vector", cA, cA, cD, ALU.mult)
            tt("vector", cB, cB, cD, ALU.mult)
            cmbv = cmbs.v(tc * nexp, nexp)
            tss("vector", sel, msk2, cB, ALU.mult)
            stt("vector", cmbv, msk1, cA, sel, ALU.mult, ALU.add)
            split_pair(cmbv, cmbh.v(tc * nexp, nexp), cmbl.v(tc * nexp, nexp))
            tt("vector", m12.v(tc * nexp, nexp), msk1, msk2, ALU.add)
        for tc in range(TC):
            pc = bank(2 + tc % 2, nexp)
            for j in range(tc + 1):
                statv = tri2.v(0, 128) if j == tc else ones2d.v(0, 128)
                mm(pc, statv, m12.v(j * nexp, nexp), j == 0, j == tc)
            vcopy("vector", cumcs.v(tc * nexp, nexp), pc)
            tt("vector", slotsS.v(tc * nexp, nexp), m12.v(tc * nexp, nexp),
               cumcs.v(tc * nexp, nexp), ALU.mult)
            tss("vector", slotsS.v(tc * nexp, nexp), slotsS.v(tc * nexp, nexp),
                -1.0, ALU.add)

        for q in range(NQ):
            ln_qt(q, mean_r, rstd_r, FMH, FML)
            for j in range(SHC):
                psu, psg = bank(0, qt), bank(1, qt)
                for pc in range(4):
                    cols = KC * 256 // 4
                    arrv = FAR.v(ARR + (pc % 3) * 1024, cols)
                    dma(arrv, dr["susg"][:, j * KC * 256 + pc * cols:
                                         j * KC * 256 + (pc + 1) * cols])
                    split_pair(arrv, BAR.v(SGH + pc * cols, cols),
                               BAR.v(SGL + pc * cols, cols),
                               hi_eng="scalar", lo_eng="vector")
                for kc in range(KC):
                    mh = BAR.v(FMH + kc * qt, qt)
                    ml = BAR.v(FML + kc * qt, qt)
                    mm3(psu, BAR.v(SGH + kc * 256, 128),
                        BAR.v(SGL + kc * 256, 128), mh, ml, kc == 0, kc == KC - 1)
                    mm3(psg, BAR.v(SGH + kc * 256 + 128, 128),
                        BAR.v(SGL + kc * 256 + 128, 128), mh, ml,
                        kc == 0, kc == KC - 1)
                t1 = FAR.v(SCA, qt)
                t2 = FAR.v(SCA + 512, qt)
                act(t1, psu, AF.Silu)
                tt("vector", t2, t1, psg, ALU.mult)
                split_pair(t2, BAR.v(HSH + j * qt, qt), BAR.v(HSL + j * qt, qt))

            def sd_out(oc, psv, q=q):
                xv = xT.v(oc * tok + q * qt, qt)
                tt("vector", xv, psv, xv, ALU.add)
            proj3p(dr["sdT"],
                   lambda kc: BAR.v(HSH + kc * qt, qt),
                   lambda kc: BAR.v(HSL + kc * qt, qt),
                   sd_out, KC, SHC, qt)

        rl1 = dma(BAR.v(TMH, TC * d), dr["tms"][:, 0:TC * d], eng="sync",
                  dep=[sp1, sp2])
        rl2 = dma(BAR.v(TML, TC * d), dr["tms"][:, TC * d:2 * TC * d],
                  eng="sync", dep=[sp1, sp2])

        for e in range(nexp):
            for tc in range(TC):
                tss("vector", sbfS.v(tc * cap, cap), iotf.v(0, cap),
                    View(slotsS.t_ap[:, tc * nexp + e:tc * nexp + e + 1],
                         slotsS.v(tc * nexp, nexp).bufs), ALU.is_equal)
            for ci, (c0, cw) in enumerate(CCS):
                for tc in range(TC):
                    ptv = PT.v((tc % 2) * 1024, 128, nr=cw)
                    transpose(ptv, View(sbfS.t_ap[:, tc * cap + c0:
                                                  tc * cap + c0 + cw],
                                        sbfS.v(tc * cap, cap).bufs),
                              identb.v(0, 128))
                    vcopy("vector", stm.v(ci * tok + tc * 128, 128, nr=cw), ptv)
                pcc = bank(0 + ci, 1, nr=cw)
                for tc in range(TC):
                    sv = View(sbfS.t_ap[:, tc * cap + c0:tc * cap + c0 + cw],
                              sbfS.v(tc * cap, cap).bufs)
                    ch = View(cmbh.t_ap[:, tc * nexp + e:tc * nexp + e + 1],
                              cmbh.v(tc * nexp, nexp).bufs)
                    cl = View(cmbl.t_ap[:, tc * nexp + e:tc * nexp + e + 1],
                              cmbl.v(tc * nexp, nexp).bufs)
                    mm(pcc, sv, ch, tc == 0, False)
                    mm(pcc, sv, cl, False, tc == TC - 1)
                vcopy("vector", cmbcol.v(ci, 1, nr=cw), pcc)
            nft = d // 512
            for kcg in range(0, KC, 4):
                for kc in range(kcg, min(kcg + 4, KC)):
                    pg = bank(2 + kc - kcg, cap)
                    sv = sbfS
                    for tc in range(TC):
                        th = BAR.v(TMH + tc * d + kc * 128, 128)
                        tl = BAR.v(TML + tc * d + kc * 128, 128)
                        smov = sbfS.v(tc * cap, cap)
                        mm(pg, th, smov, tc == 0, False)
                        mm(pg, tl, smov, False, tc == TC - 1)
                    vcopy("vector", FAR.v(FGT + kc * cap, cap), pg)
            for wname in ("wu", "wg"):
                dramw = dr[wname]
                ngrp = (EFC + 3) // 4
                for g2 in range(ngrp):
                    gw = min(4, EFC - g2 * 4)
                    base = g2 * KC * 512
                    for pc in range(KC // 2):
                        cols = 2 * gw * 128
                        arrv = FAR.v(ARR + (pc % 3) * 1024, cols)
                        dma(arrv, dramw[e, :, base + pc * 2 * gw * 128:
                                        base + (pc + 1) * 2 * gw * 128])
                        for k in range(2):
                            kc = pc * 2 + k
                            for oi in range(gw):
                                sv = FAR.v(ARR + (pc % 3) * 1024 +
                                           (k * gw + oi) * 128, 128)
                                mm(bank(2 + oi, cap), sv, FAR.v(FGT + kc * cap, cap),
                                   kc == 0, kc == KC - 1)
                    for oi in range(gw):
                        efc = g2 * 4 + oi
                        if wname == "wu":
                            vcopy("vector", FAR.v(FU + efc * cap, cap),
                                  bank(2 + oi, cap))
                        else:
                            t1 = FAR.v(SCA, cap)
                            act(t1, FAR.v(FU + efc * cap, cap), AF.Silu)
                            tt("vector", FAR.v(FH + efc * cap, cap), t1,
                               bank(2 + oi, cap), ALU.mult)
            for ftg in range(0, nft, 2):
                nfi = min(2, nft - ftg)
                for kcd in range(EFC):
                    arrv = FAR.v(ARR + (kcd % 3) * 1024, nfi * 512)
                    dma(arrv, dr["wd"][e, :, kcd * d + ftg * 512:
                                       kcd * d + (ftg + nfi) * 512])
                    for fi in range(nfi):
                        for ci, (c0, cw) in enumerate(CCS):
                            pd = bank(2 + fi * 2 + ci, 512, nr=cw)
                            sv = FAR.v(FH + kcd * cap + c0, cw)
                            mv = FAR.v(ARR + (kcd % 3) * 1024 + fi * 512, 512)
                            mm(pd, sv, mv, kcd == 0, kcd == EFC - 1)
                for fi in range(nfi):
                    ft = ftg + fi
                    eo0 = EOS + (ft % 2) * 2048
                    for ci, (c0, cw) in enumerate(CCS):
                        pd = bank(2 + fi * 2 + ci, 512, nr=cw)
                        t1 = FAR.v(SCB + (ci % 2) * 512, 512, nr=cw)
                        tss("vector", t1, pd, cmbcol.v(ci, 1, nr=cw), ALU.mult)
                        split_pair(t1, BAR.v(eo0 + ci * 1024, 512, nr=cw),
                                   BAR.v(eo0 + ci * 1024 + 512, 512, nr=cw))
                    for fl in range(4):
                        fc = ft * 4 + fl
                        for q2 in range(NQ2):
                            psc = bank(q2, 512)
                            first = True
                            for ci, (c0, cw) in enumerate(CCS):
                                eoh = BAR.v(eo0 + ci * 1024 + fl * 128, 128, nr=cw)
                                eol = BAR.v(eo0 + ci * 1024 + 512 + fl * 128,
                                            128, nr=cw)
                                smv = stm.v(ci * tok + q2 * 512, 512, nr=cw)
                                mm(psc, eoh, smv, first, False)
                                mm(psc, eol, smv, False, ci == len(CCS) - 1)
                                first = False
                            xv = xT.v(fc * tok + q2 * 512, 512)
                            tt("vector", xv, psc, xv, ALU.add)

        outs = []
        for c in range(KC * tok // XSTEP):
            outs.append(dma_out(dr["yT"][:, c * XSTEP:(c + 1) * XSTEP],
                                xT.v(c * XSTEP, XSTEP)))
        kb.op("gpsimd", None, dep=outs)

        kb.generate()
    return nc


def prepare_maps(inputs, cfg, ncores):
    tok, d, heads = cfg["tok"], cfg["d"], cfg["heads"]
    nexp, shef, ef = cfg["nexp"], cfg["shef"], cfg["ef"]
    KC = d // 128

    x = np.asarray(inputs["L0_attn_query"], dtype=np.float32)
    kv_k = np.asarray(inputs["L0_attn_kv_k"], dtype=np.float32)
    kv_v = np.asarray(inputs["L0_attn_kv_v"], dtype=np.float32)
    Wq = np.asarray(inputs["L0_attn_Wq"], dtype=np.float32)
    Wo = np.asarray(inputs["L0_attn_Wo"], dtype=np.float32)
    rw = np.asarray(inputs["L0_ffn_router_w"], dtype=np.float32)
    rb = np.asarray(inputs["L0_ffn_router_bias"], dtype=np.float32)
    Wu = np.asarray(inputs["L0_ffn_Wu"], dtype=np.float32)
    Wg = np.asarray(inputs["L0_ffn_Wg"], dtype=np.float32)
    Wd = np.asarray(inputs["L0_ffn_Wd"], dtype=np.float32)
    Su = np.asarray(inputs["L0_ffn_Su"], dtype=np.float32)[0]
    Sg = np.asarray(inputs["L0_ffn_Sg"], dtype=np.float32)[0]
    Sd = np.asarray(inputs["L0_ffn_Sd"], dtype=np.float32)[0]

    B, T, D = x.shape
    ntok = B * T
    xf = x.reshape(ntok, D)

    wqT = pack_proj(np.ascontiguousarray(Wq.T), 256)
    woT = pack_proj(np.ascontiguousarray(Wo.T), 256)
    kvt = kv_k.reshape(-1, heads, 128).transpose(1, 2, 0)
    kT = np.ascontiguousarray(np.concatenate(
        [kvt[h] for h in range(heads)], axis=1))
    vvt = kv_v.reshape(-1, heads, 128).transpose(1, 0, 2)
    vv = np.ascontiguousarray(np.concatenate(
        [vvt[h].reshape(-1, 128, 128).transpose(1, 0, 2).reshape(128, -1)
         for h in range(heads)], axis=1))
    rwT = fm(np.ascontiguousarray(rw.T))
    rbr = np.ascontiguousarray(rb.reshape(1, nexp))
    egc = min(4, ef // 128) * 128
    wu_p = np.stack([pack_proj(Wu[e], egc) for e in range(nexp)])
    wg_p = np.stack([pack_proj(Wg[e], egc) for e in range(nexp)])
    wd_p = np.stack([fm(Wd[e]) for e in range(nexp)])
    SHC = shef // 128
    blocks = []
    for j in range(SHC):
        for kc in range(KC):
            blocks.append(np.concatenate(
                [Su[kc * 128:(kc + 1) * 128, j * 128:(j + 1) * 128],
                 Sg[kc * 128:(kc + 1) * 128, j * 128:(j + 1) * 128]], axis=1))
    susg = np.ascontiguousarray(np.concatenate(blocks, axis=1))
    sdT = pack_proj(np.ascontiguousarray(Sd), 256)

    weights = dict(wqT=wqT, woT=woT, kT=kT, vv=vv, rwT=rwT, rb=rbr,
                   wu=wu_p, wg=wg_p, wd=wd_p, susg=susg, sdT=sdT)

    per = ntok // ncores
    in_maps = []
    for c in range(ncores):
        xs = np.ascontiguousarray(xf[c * per:(c + 1) * per].T)
        m = dict(weights)
        m["xT"] = fm(xs)
        in_maps.append(m)
    return in_maps, (B, T, D)


def kernel(**inputs):
    cfg = dict(FULL)
    in_maps, (B, T, D) = prepare_maps(inputs, cfg, NCORES)
    per = B * T // NCORES
    nc = build(cfg)
    res = run_bass_kernel_spmd(nc, in_maps, core_ids=list(range(NCORES)))
    outs = []
    for c in range(NCORES):
        yT = np.asarray(res.results[c]["yT"])
        outs.append(unfm(yT, D, per).T)
    return np.concatenate(outs, axis=0).reshape(B, T, D).astype(np.float32)


# version 18
# speedup vs baseline: 1.3876x; 1.0632x over previous
import contextlib

import numpy as np

import concourse.bass as bass
import concourse.mybir as mybir
from concourse.bass_utils import run_bass_kernel_spmd

F32 = mybir.dt.float32
BF16 = mybir.dt.bfloat16
I32 = mybir.dt.int32
AF = mybir.ActivationFunctionType
ALU = mybir.AluOpType
AX = mybir.AxisListType

NCORES = 8

FULL = dict(tok=1024, d=2048, heads=16, hd=128, kv=4096, nexp=16, ef=1024,
            cap=192, shef=2048, qt=256)

ENGS = ("sync", "scalar", "vector", "tensor", "gpsimd")


class Op:
    __slots__ = ("eng", "fn", "deps", "dma", "needed", "val", "waits", "tag",
                 "slot")

    def __init__(self, eng, fn, dma, tag):
        self.eng = eng
        self.fn = fn
        self.deps = []
        self.dma = dma
        self.needed = False
        self.val = None
        self.waits = []
        self.tag = tag
        self.slot = 0

    @property
    def semkey(self):
        return f"{self.eng}_dma{self.slot}" if self.dma else self.eng


class Buf:
    __slots__ = ("name", "writer", "readers")

    def __init__(self, name):
        self.name = name
        self.writer = None
        self.readers = []


class View:
    __slots__ = ("ap", "bufs")

    def __init__(self, ap, bufs):
        self.ap = ap
        self.bufs = list(bufs)


class Slab:

    def __init__(self, name, ap, blk_cols, ncols, nrows=128):
        self.name = name
        self.t_ap = ap
        self.blk = blk_cols
        self.ncols = ncols
        self.nrows = nrows
        nblk = (ncols + blk_cols - 1) // blk_cols
        self.bufs = [Buf(f"{name}[{i}]") for i in range(nblk)]

    def v(self, c0, nc, r0=0, nr=None):
        nr = self.nrows if nr is None else nr
        assert 0 <= c0 and c0 + nc <= self.ncols, (self.name, c0, nc)
        b0 = c0 // self.blk
        b1 = (c0 + nc - 1) // self.blk
        return View(self.t_ap[r0:r0 + nr, c0:c0 + nc], self.bufs[b0:b1 + 1])


DMA_SLOTS = {"sync": 8, "scalar": 4, "vector": 4, "tensor": 4, "gpsimd": 4}


class KB:
    def __init__(self, nc, es):
        self.nc = nc
        self.es = es
        self.ops = {e: [] for e in ENGS}
        self.sems = {}
        self.dmacnt = {e: 0 for e in ENGS}
        self.lastdma = {}

    def op(self, eng, fn, r=(), w=(), dep=(), dma=False, tag=""):
        o = Op(eng, fn, dma, tag)
        if dma:
            o.slot = self.dmacnt[eng] % DMA_SLOTS[eng]
            self.dmacnt[eng] += 1
            prev = self.lastdma.get(o.semkey)
            if prev is not None:
                dep = list(dep) + [prev]
            self.lastdma[o.semkey] = o
        deps = []
        for vw in r:
            for b in vw.bufs:
                if b.writer is not None:
                    deps.append(b.writer)
        for vw in w:
            for b in vw.bufs:
                if b.writer is not None:
                    deps.append(b.writer)
                deps.extend(b.readers)
        deps.extend(dep)
        seen = set()
        for d in deps:
            if d is o or id(d) in seen:
                continue
            if d.eng == "tensor" and eng == "tensor" and not d.dma and not dma:
                continue
            seen.add(id(d))
            o.deps.append(d)
        for vw in r:
            for b in vw.bufs:
                b.readers.append(o)
        for vw in w:
            for b in vw.bufs:
                b.writer = o
                b.readers = []
        self.ops[eng].append(o)
        return o

    def generate(self):
        for e in ENGS:
            for o in self.ops[e]:
                for d in o.deps:
                    d.needed = True
        counters = {}
        for e in ENGS:
            for o in self.ops[e]:
                if o.needed:
                    k = o.semkey
                    counters[k] = counters.get(k, 0) + (16 if o.dma else 1)
                    o.val = counters[k]
        for k in counters:
            self.sems[k] = self.es.enter_context(self.nc.semaphore("S" + k))
        water = {e: {} for e in ENGS}
        for e in ENGS:
            for o in self.ops[e]:
                need = {}
                for d in o.deps:
                    k = d.semkey
                    if d.val > need.get(k, 0):
                        need[k] = d.val
                for k, v in sorted(need.items()):
                    if v > water[e].get(k, 0):
                        o.waits.append((k, v))
                        water[e][k] = v
        nc = self.nc
        with nc.Block() as blk:
            for engname in ENGS:
                ops = self.ops[engname]

                def runner(eng, ops=ops, self=self):
                    for o in ops:
                        for (k, v) in o.waits:
                            eng.wait_ge(self.sems[k], v)
                        if o.fn is not None:
                            inst = o.fn(eng)
                            if o.needed:
                                inst.then_inc(self.sems[o.semkey],
                                              16 if o.dma else 1)
                getattr(blk, engname)(runner)


def fm(a):
    r, c = a.shape
    assert r % 128 == 0
    return np.ascontiguousarray(
        a.reshape(r // 128, 128, c).transpose(1, 0, 2).reshape(128, -1))


def unfm(a, r, c):
    return np.ascontiguousarray(
        a.reshape(128, r // 128, c).transpose(1, 0, 2).reshape(r, c))


def pack_proj(wT, ocg_cols):
    k, o = wT.shape
    kc = k // 128
    ogn = o // ocg_cols
    blocks = []
    for og in range(ogn):
        for r in range(kc):
            blocks.append(wT[r * 128:(r + 1) * 128,
                             og * ocg_cols:(og + 1) * ocg_cols])
    return np.ascontiguousarray(np.concatenate(blocks, axis=1))


def build(cfg, stop_after=None):
    tok, d, heads = cfg["tok"], cfg["d"], cfg["heads"]
    kv, nexp, ef, cap = cfg["kv"], cfg["nexp"], cfg["ef"], cfg["cap"]
    shef, qt = cfg["shef"], cfg["qt"]
    KC = d // 128
    TC = tok // 128
    NQ = tok // qt
    KVC = kv // 128
    EFC = ef // 128
    SHC = shef // 128
    NQ2 = tok // 512
    KQ = kv // 4
    KQC = KQ // 128
    CAPM = max(cap, 128)
    SCALE = float(np.float32(1.0 / np.sqrt(np.float64(cfg["hd"]))))
    EPS = 1e-5
    CCS = [(0, min(128, cap))] + ([(128, cap - 128)] if cap > 128 else [])
    OCG = 2

    nc = bass.Bass("TRN2", target_bir_lowering=False, debug=False)

    dr = {}
    dr["xT"] = nc.dram_tensor("xT", [128, KC * tok], F32, kind="ExternalInput").ap()
    dr["wqT"] = nc.dram_tensor("wqT", [128, KC * d], F32, kind="ExternalInput").ap()
    dr["woT"] = nc.dram_tensor("woT", [128, KC * d], F32, kind="ExternalInput").ap()
    dr["kT"] = nc.dram_tensor("kT", [128, heads * kv], F32, kind="ExternalInput").ap()
    dr["vv"] = nc.dram_tensor("vv", [128, heads * kv], F32, kind="ExternalInput").ap()
    dr["rwT"] = nc.dram_tensor("rwT", [128, KC * nexp], F32, kind="ExternalInput").ap()
    dr["rb"] = nc.dram_tensor("rb", [1, nexp], F32, kind="ExternalInput").ap()
    dr["wu"] = nc.dram_tensor("wu", [nexp, 128, KC * ef], F32, kind="ExternalInput").ap()
    dr["wg"] = nc.dram_tensor("wg", [nexp, 128, KC * ef], F32, kind="ExternalInput").ap()
    dr["wd"] = nc.dram_tensor("wd", [nexp, 128, EFC * d], F32, kind="ExternalInput").ap()
    dr["susg"] = nc.dram_tensor("susg", [128, SHC * KC * 256], F32, kind="ExternalInput").ap()
    dr["sdT"] = nc.dram_tensor("sdT", [128, SHC * d], F32, kind="ExternalInput").ap()
    dr["yT"] = nc.dram_tensor("yT", [128, KC * tok], F32, kind="ExternalOutput").ap()
    dr["tms"] = nc.dram_tensor("tms", [128, 2 * TC * d], BF16, kind="Internal").ap()

    with contextlib.ExitStack() as es:
        E = es.enter_context
        kb = KB(nc, es)

        def sb(name, shape, dt):
            return E(nc.sbuf_tensor(name, shape, dt)).ap()

        xT = Slab("xT", sb("s_xT", [128, KC * tok], F32), 512, KC * tok)
        FARN = 11264
        FAR = Slab("FAR", sb("s_FAR", [128, FARN], F32), 512, FARN)

        A = KC * qt
        WPN = KC * OCG * 128
        QUAD = min(4, KVC)
        KQB = QUAD * 128
        NPSLOT = 3
        XR = max(2 * WPN, 12 * KQB + NPSLOT * 2 * QUAD * qt)
        attn_need = 6 * A + XR
        ln2_need = 2 * TC * d + 2 * tok
        shared_need = 2 * KC * min(512, tok) + 2 * SHC * min(512, tok) + 4096
        exp_need = 2 * TC * d + 2 * 4 * 512
        BARN = max(attn_need, ln2_need, shared_need, exp_need)
        BAR = Slab("BAR", sb("s_BAR", [128, BARN], BF16), 512, BARN)

        LNH, LNL, QH, QL, AOH, AOL = (i * A for i in range(6))
        WPH, WPL = 6 * A, 6 * A + WPN
        KHQ, KLQ, VHQ, VLQ = (6 * A + i * 3 * KQB for i in range(4))
        PB = 6 * A + 12 * KQB
        TMH, TML, PCH, PCL = 0, TC * d, 2 * TC * d, 2 * TC * d + tok
        SQ = min(512, tok)
        FMH, FML = 0, KC * SQ
        HSH, HSL = 2 * KC * SQ, 2 * KC * SQ + SHC * SQ
        SGB = 2 * KC * SQ + 2 * SHC * SQ
        EOS = 2 * TC * d

        ioti = Slab("ioti", sb("s_ioti", [128, CAPM], I32), CAPM, CAPM)
        iotci = Slab("iotci", sb("s_iotci", [128, 1], I32), 1, 1)
        iotf = Slab("iotf", sb("s_iotf", [128, CAPM], F32), CAPM, CAPM)
        iotc = Slab("iotc", sb("s_iotc", [128, 1], F32), 1, 1)
        identf = Slab("identf", sb("s_identf", [128, 128], F32), 128, 128)
        identb = Slab("identb", sb("s_identb", [128, 128], BF16), 128, 128)
        tri2 = Slab("tri2", sb("s_tri2", [128, 128], F32), 128, 128)
        ones2d = Slab("ones2d", sb("s_ones2d", [128, 128], F32), 128, 128)
        onesc = Slab("onesc", sb("s_onesc", [128, 1], F32), 1, 1)
        onescb = Slab("onescb", sb("s_onescb", [128, 1], BF16), 1, 1)
        onesr = Slab("onesr", sb("s_onesr", [1, 128], F32), 128, 128, nrows=1)

        rows = Slab("rows", sb("s_rows", [1, 3072], F32), 512, 3072, nrows=1)
        cols4 = Slab("cols4", sb("s_cols4", [128, 4], F32), 1, 4)
        rwTs = Slab("rwTs", sb("s_rwTs", [128, KC * nexp], F32), 512, KC * nexp)
        rbrow = Slab("rbrow", sb("s_rbrow", [1, nexp], F32), nexp, nexp, nrows=1)
        biasbc = Slab("biasbc", sb("s_biasbc", [128, nexp], F32), nexp, nexp)
        logtm = Slab("logtm", sb("s_logtm", [128, TC * nexp], F32), nexp, TC * nexp)
        m12 = Slab("m12", sb("s_m12", [128, TC * nexp], F32), nexp, TC * nexp)
        cmbs = Slab("cmbs", sb("s_cmbs", [128, TC * nexp], F32), nexp, TC * nexp)
        cmbh = Slab("cmbh", sb("s_cmbh", [128, TC * nexp], BF16), nexp, TC * nexp)
        cmbl = Slab("cmbl", sb("s_cmbl", [128, TC * nexp], BF16), nexp, TC * nexp)
        cumcs = Slab("cumcs", sb("s_cumcs", [128, TC * nexp], F32), nexp, TC * nexp)
        slotsS = Slab("slots", sb("s_slots", [128, TC * nexp], F32), nexp, TC * nexp)
        smsk = Slab("smsk", sb("s_smsk", [128, 5 * nexp], F32), nexp, 5 * nexp)
        sbfS = Slab("sbf", sb("s_sbf", [128, TC * cap], BF16), cap, TC * cap)
        stm = Slab("stm", sb("s_stm", [128, len(CCS) * tok], BF16), 512, len(CCS) * tok)
        cmbcol = Slab("cmbcol", sb("s_cmbcol", [128, len(CCS)], F32), 1, len(CCS))

        PS = Slab("PS", E(nc.psum_tensor("p_PS", [128, 3072], F32)).ap(), 512, 3072)
        PT = Slab("PT", E(nc.psum_tensor("p_PT", [128, 2048], BF16)).ap(), 1024, 2048)

        def bank(b, cols=512, r0=0, nr=128):
            return PS.v(b * 512, cols, r0, nr)

        SCA, SCB = 0, 1024
        PF = [2048, 2560]
        RBF, SCC = 3072, 3584
        KARR, VARR, ARR = 4096, 6144, 8192
        FH, FU = 2048, 2048 + EFC * cap
        FGT = FU + EFC * cap
        assert FGT + KC * cap <= ARR, (FGT + KC * cap, ARR)
        assert ARR + 3 * 1024 <= FARN

        def mm(psv, statv, movv, start, stop, dep=()):
            return kb.op("tensor", lambda T, o=psv.ap, l=statv.ap, r=movv.ap,
                         a=start, b=stop: T.matmul(o, l, r, start=a, stop=b),
                         r=[statv, movv], w=[psv], dep=dep, tag="mm")

        def mm3(psv, sh, sl, mh, ml, start, stop):
            mm(psv, sh, mh, start, False)
            mm(psv, sh, ml, False, False)
            mm(psv, sl, mh, False, stop)

        def vcopy(eng, dst, src, dep=()):
            kb.op(eng, lambda Eo, o=dst.ap, i=src.ap: Eo.tensor_copy(out=o, in_=i),
                  r=[src], w=[dst], dep=dep, tag="copy")

        def scopy(dst, src, dep=()):
            kb.op("scalar", lambda Eo, o=dst.ap, i=src.ap: Eo.copy(o, i),
                  r=[src], w=[dst], dep=dep, tag="scopy")

        def tt(eng, dst, a, b, opr, dep=()):
            kb.op(eng, lambda Eo, o=dst.ap, x=a.ap, y=b.ap, p=opr:
                  Eo.tensor_tensor(out=o, in0=x, in1=y, op=p),
                  r=[a, b], w=[dst], dep=dep, tag="tt")

        def tss(eng, dst, a, scal, opr, dep=()):
            if isinstance(scal, View):
                kb.op(eng, lambda Eo, o=dst.ap, x=a.ap, s=scal.ap, p=opr:
                      Eo.tensor_single_scalar(out=o, in_=x, scalar=s, op=p),
                      r=[a, scal], w=[dst], dep=dep, tag="tss")
            else:
                kb.op(eng, lambda Eo, o=dst.ap, x=a.ap, s=scal, p=opr:
                      Eo.tensor_single_scalar(out=o, in_=x, scalar=s, op=p),
                      r=[a], w=[dst], dep=dep, tag="tss")

        def stt(eng, dst, in0, scal, in1, op0, op1, dep=()):
            sc = scal.ap if isinstance(scal, View) else scal
            rr = [in0, in1] + ([scal] if isinstance(scal, View) else [])
            kb.op(eng, lambda Eo, o=dst.ap, x=in0.ap, s=sc, y=in1.ap,
                  p0=op0, p1=op1:
                  Eo.scalar_tensor_tensor(out=o, in0=x, scalar=s, in1=y,
                                          op0=p0, op1=p1),
                  r=rr, w=[dst], dep=dep, tag="stt")

        def act(dst, src, func, scale=1.0, bias=0.0, dep=()):
            bap = bias.ap if isinstance(bias, View) else bias
            rr = [src] + ([bias] if isinstance(bias, View) else [])
            kb.op("scalar", lambda Eo, o=dst.ap, i=src.ap, f=func, s=scale,
                  b=bap: Eo.activation(o, i, f, bias=b, scale=s),
                  r=rr, w=[dst], dep=dep, tag="act")

        def red(dst, src, opr, dep=()):
            kb.op("vector", lambda Eo, o=dst.ap, i=src.ap, p=opr:
                  Eo.tensor_reduce(out=o, in_=i, axis=AX.X, op=p),
                  r=[src], w=[dst], dep=dep, tag="red")

        def recip(dst, src, dep=()):
            kb.op("vector", lambda Eo, o=dst.ap, i=src.ap:
                  Eo.reciprocal(out=o, in_=i), r=[src], w=[dst], dep=dep,
                  tag="recip")

        def memset(eng, dst, val, dep=()):
            kb.op(eng, lambda Eo, o=dst.ap, v=val: Eo.memset(o, v),
                  r=[], w=[dst], dep=dep, tag="memset")

        def dma(dst, src_ap, eng="sync", dep=()):
            return kb.op(eng, lambda Eo, o=dst.ap, i=src_ap:
                         Eo.dma_start(out=o, in_=i),
                         r=[], w=[dst], dep=dep, dma=True, tag="dma")

        def dma_out(dst_ap, src, eng="scalar", dep=()):
            return kb.op(eng, lambda Eo, o=dst_ap, i=src.ap:
                         Eo.dma_start(out=o, in_=i),
                         r=[src], w=[], dep=dep, dma=True, tag="dmao")

        def transpose(psv, src, ident, dep=()):
            kb.op("tensor", lambda T, o=psv.ap, i=src.ap, dd=ident.ap:
                  T.transpose(o, i, dd), r=[src, ident], w=[psv], dep=dep,
                  tag="tr")

        def split_pair(src, dsth, dstl, hi_eng="scalar", lo_eng="vector"):
            if hi_eng == "scalar":
                scopy(dsth, src)
            elif hi_eng == "dma":
                kb.op("gpsimd", lambda Eo, o=dsth.ap, i=src.ap:
                      Eo.dma_start(out=o, in_=i),
                      r=[src], w=[dsth], dma=True, tag="cdma")
            else:
                vcopy(hi_eng, dsth, src)
            tt(lo_eng, dstl, src, dsth, ALU.subtract)

        memset("vector", onesc.v(0, 1), 1.0)
        memset("vector", onescb.v(0, 1), 1.0)
        memset("vector", onesr.v(0, 128), 1.0)
        memset("vector", ones2d.v(0, 128), 1.0)
        kb.op("gpsimd", lambda G, o=iotci.v(0, 1).ap:
              G.iota(o, pattern=[[0, 1]], channel_multiplier=1),
              w=[iotci.v(0, 1)], tag="iota")
        kb.op("gpsimd", lambda G, o=ioti.v(0, CAPM).ap:
              G.iota(o, pattern=[[1, CAPM]], channel_multiplier=0),
              w=[ioti.v(0, CAPM)], tag="iota")
        vcopy("vector", iotc.v(0, 1), iotci.v(0, 1))
        vcopy("vector", iotf.v(0, CAPM), ioti.v(0, CAPM))
        tss("vector", identf.v(0, 128), iotf.v(0, 128), iotc.v(0, 1), ALU.is_equal)
        tss("vector", identb.v(0, 128), iotf.v(0, 128), iotc.v(0, 1), ALU.is_equal)
        tss("vector", tri2.v(0, 128), iotf.v(0, 128), iotc.v(0, 1), ALU.is_ge)
        dma(rwTs.v(0, KC * nexp), dr["rwT"][:, :])
        dma(rbrow.v(0, nexp), dr["rb"][:, :])
        mm(bank(0, nexp), onesr.v(0, 128), rbrow.v(0, nexp), True, True)
        vcopy("vector", biasbc.v(0, nexp), bank(0, nexp))
        XSTEP = min(4096, KC * tok)
        for c in range(KC * tok // XSTEP):
            dma(xT.v(c * XSTEP, XSTEP), dr["xT"][:, c * XSTEP:(c + 1) * XSTEP],
                eng="scalar")

        def ln_stats(mean_r, rstd_r):
            s1 = [bank(i, 512, nr=1) for i in range(NQ2)]
            s2 = [bank(NQ2 + i, 512, nr=1) for i in range(NQ2)]
            for kc in range(KC):
                sq0 = SCA if kc % 2 == 0 else SCB
                for q2 in range(NQ2):
                    xv = xT.v(kc * tok + q2 * 512, 512)
                    sqv = FAR.v(sq0 + (q2 % 2) * 512, 512)
                    act(sqv, xv, AF.Square)
                    mm(s1[q2], onesc.v(0, 1), xv, kc == 0, kc == KC - 1)
                    mm(s2[q2], onesc.v(0, 1), sqv, kc == 0, kc == KC - 1)
            r_s1 = FAR.v(KARR, tok, nr=1)
            r_s2 = FAR.v(KARR + 1024, tok, nr=1)
            for q2 in range(NQ2):
                vcopy("vector", FAR.v(KARR + q2 * 512, 512, nr=1), s1[q2])
                vcopy("vector", FAR.v(KARR + 1024 + q2 * 512, 512, nr=1), s2[q2])
            tA = FAR.v(VARR, tok, nr=1)
            tB = FAR.v(VARR + 1024, tok, nr=1)
            tss("vector", mean_r, r_s1, 1.0 / d, ALU.mult)
            tss("vector", tA, r_s2, 1.0 / d, ALU.mult)
            tt("vector", tB, mean_r, mean_r, ALU.mult)
            tt("vector", tA, tA, tB, ALU.subtract)
            tss("vector", tA, tA, EPS, ALU.add)
            recip(tB, tA)
            act(rstd_r, tB, AF.Sqrt)

        def ln_qt(q, mean_r, rstd_r, oh, ol, w=qt):
            pmb, prb = bank(0, w), bank(1, w)
            mm(pmb, onesr.v(0, 128), View(mean_r.ap[0:1, q * w:(q + 1) * w],
                                        mean_r.bufs), True, True)
            mm(prb, onesr.v(0, 128), View(rstd_r.ap[0:1, q * w:(q + 1) * w],
                                        rstd_r.bufs), True, True)
            for kc in range(KC):
                xv = xT.v(kc * tok + q * w, w)
                t1 = FAR.v(SCA + (kc % 2) * 512, w)
                t2 = FAR.v(SCB + (kc % 2) * 512, w)
                tt("vector", t1, xv, pmb, ALU.subtract)
                tt("vector", t2, t1, prb, ALU.mult)
                split_pair(t2, BAR.v(oh + kc * w, w), BAR.v(ol + kc * w, w))

        def proj3p(dram_ap, movh_fn, movl_fn, out_fn, OCn, kcn, N,
                   wph=None, wpl=None):
            wph = WPH if wph is None else wph
            wpl = WPL if wpl is None else wpl
            pk = max(1, 1024 // (OCG * 128))
            npieces = (kcn + pk - 1) // pk
            ogn = OCn // OCG
            for og in range(ogn):
                base = og * kcn * OCG * 128
                for pc in range(npieces):
                    k0 = pc * pk
                    k1 = min(kcn, k0 + pk)
                    cols = (k1 - k0) * OCG * 128
                    arrv = FAR.v(ARR + (pc % 3) * 1024, cols)
                    dma(arrv, dram_ap[:, base + k0 * OCG * 128:
                                      base + k1 * OCG * 128])
                    wh = BAR.v(wph + k0 * OCG * 128, cols)
                    wl = BAR.v(wpl + k0 * OCG * 128, cols)
                    split_pair(arrv, wh, wl, hi_eng="scalar", lo_eng="vector")
                    for kc in range(k0, k1):
                        for oi in range(OCG):
                            sh = BAR.v(wph + (kc * OCG + oi) * 128, 128)
                            sl = BAR.v(wpl + (kc * OCG + oi) * 128, 128)
                            mm3(bank((og % 2) * OCG + oi, N), sh, sl,
                                movh_fn(kc), movl_fn(kc),
                                kc == 0, kc == kcn - 1)
                for oi in range(OCG):
                    out_fn(og * OCG + oi, bank((og % 2) * OCG + oi, N))

        mean_r = rows.v(0, tok)
        rstd_r = rows.v(1024, tok)
        ln_stats(mean_r, rstd_r)

        for q in range(NQ):
            ln_qt(q, mean_r, rstd_r, LNH, LNL)

            def q_out(oc, psv):
                split_pair(psv, BAR.v(QH + oc * qt, qt), BAR.v(QL + oc * qt, qt))
            proj3p(dr["wqT"],
                   lambda kc: BAR.v(LNH + kc * qt, qt),
                   lambda kc: BAR.v(LNL + kc * qt, qt),
                   q_out, KC, KC, qt)

            NQD = KVC // QUAD
            pqk2 = [PS.v(0, QUAD * qt), PS.v(QUAD * qt, QUAD * qt)]
            pav = bank(4, qt)
            pr = PS.v(2560, 2 * qt, nr=1)
            prb = bank(0, qt)
            tail_fns = []
            for h in range(heads):
                def kv_dma(kq, h=h):
                    ka = FAR.v(KARR + (kq % 3) * 512, KQB)
                    va = FAR.v(VARR + (kq % 3) * 512, KQB)
                    dma(ka, dr["kT"][:, h * kv + kq * KQB:
                                     h * kv + (kq + 1) * KQB])
                    dma(va, dr["vv"][:, h * kv + kq * KQB:
                                     h * kv + (kq + 1) * KQB])

                def kv_split(kq, h=h):
                    s = (kq % 3) * KQB
                    ka = FAR.v(KARR + (kq % 3) * 512, KQB)
                    va = FAR.v(VARR + (kq % 3) * 512, KQB)
                    split_pair(ka, BAR.v(KHQ + s, KQB), BAR.v(KLQ + s, KQB),
                               hi_eng="scalar", lo_eng="gpsimd")
                    split_pair(va, BAR.v(VHQ + s, KQB), BAR.v(VLQ + s, KQB),
                               hi_eng="scalar", lo_eng="gpsimd")
                if h == 0:
                    kv_dma(0)
                    kv_dma(1)
                    kv_split(0)
                qh = BAR.v(QH + h * qt, qt)
                ql = BAR.v(QL + h * qt, qt)
                pend = []

                def av_mms(kq, h=h):
                    s_p = (kq % NPSLOT) * 2 * QUAD * qt
                    s = (kq % 3) * KQB
                    for kc in range(kq * QUAD, (kq + 1) * QUAD):
                        off = (kc % QUAD) * 128
                        vh = BAR.v(VHQ + s + off, 128)
                        vl = BAR.v(VLQ + s + off, 128)
                        phv = BAR.v(PB + s_p + (kc % QUAD) * qt, qt)
                        plv = BAR.v(PB + s_p + QUAD * qt + (kc % QUAD) * qt, qt)
                        mm3(pav, vh, vl, phv, plv, kc == 0, kc == KVC - 1)
                    for j in range(2 * QUAD * qt // 512):
                        mm(pr, onescb.v(0, 1),
                           BAR.v(PB + s_p + j * 512, 512),
                           kq == 0 and j == 0,
                           kq == NQD - 1 and j == 2 * QUAD * qt // 512 - 1)

                for kq in range(NQD):
                    if len(pend) > 1:
                        av_mms(pend.pop(0))
                    if kq == 1 and tail_fns:
                        tail_fns.pop(0)()
                    pqk = pqk2[kq % 2]
                    s = (kq % 3) * KQB
                    if kq + 1 < NQD:
                        kv_split(kq + 1)
                    if kq + 2 < NQD:
                        kv_dma(kq + 2)
                    elif h + 1 < heads and kq + 2 - NQD < 2:
                        kv_dma(kq + 2 - NQD, h=h + 1)
                    for kc in range(kq * QUAD, (kq + 1) * QUAD):
                        off = (kc % QUAD) * 128
                        kh = BAR.v(KHQ + s + off, 128)
                        kl = BAR.v(KLQ + s + off, 128)
                        pq = View(pqk.ap[:, (kc % QUAD) * qt:
                                         (kc % QUAD) * qt + qt], pqk.bufs)
                        mm3(pq, kh, kl, qh, ql, (kc % 2) == 0, (kc % 2) == 1)
                    pf = FAR.v(SCA + (kq % 2) * 1024, QUAD * qt)
                    act(pf, pqk, AF.Exp, scale=SCALE)
                    s_p = (kq % NPSLOT) * 2 * QUAD * qt
                    phv = BAR.v(PB + s_p, QUAD * qt)
                    plv = BAR.v(PB + s_p + QUAD * qt, QUAD * qt)
                    vcopy("vector", phv, pf)
                    tt("vector", plv, pf, phv, ALU.subtract)
                    pend.append(kq)
                for pp in pend:
                    av_mms(pp)
                if h + 1 < heads:
                    kv_split(0, h=h + 1)
                rsum = rows.v(2048, 2 * qt)
                vcopy("vector", rsum, pr)
                rec = rows.v(2048 + 2 * qt, qt)
                tt("vector", rec, rows.v(2048, qt), rows.v(2048 + qt, qt),
                   ALU.add)
                recip(rec, rec)

                def tail(h=h, rec=rec):
                    mm(prb, onesr.v(0, 128), rec, True, True)
                    rbf = FAR.v(RBF, qt)
                    vcopy("vector", rbf, prb)
                    t1 = FAR.v(SCC, qt)
                    tt("vector", t1, pav, rbf, ALU.mult)
                    split_pair(t1, BAR.v(AOH + h * qt, qt),
                               BAR.v(AOL + h * qt, qt))
                tail_fns.append(tail)
                if (isinstance(stop_after, tuple) and stop_after[0] == "head"
                        and q == stop_after[1] and h == stop_after[2]):
                    while tail_fns:
                        tail_fns.pop(0)()
                    kb.generate()
                    return nc
            while tail_fns:
                tail_fns.pop(0)()

            def wo_out(oc, psv, q=q):
                xv = xT.v(oc * tok + q * qt, qt)
                tt("vector", xv, psv, xv, ALU.add)
            proj3p(dr["woT"],
                   lambda kc: BAR.v(AOH + kc * qt, qt),
                   lambda kc: BAR.v(AOL + kc * qt, qt),
                   wo_out, KC, KC, qt)

        if stop_after == "attn":
            outs = []
            XS = min(4096, KC * tok)
            for c in range(KC * tok // XS):
                outs.append(dma_out(dr["yT"][:, c * XS:(c + 1) * XS],
                                    xT.v(c * XS, XS)))
            kb.op("gpsimd", None, dep=outs)
            kb.generate()
            return nc

        ln_stats(mean_r, rstd_r)
        pmb2 = [bank(i, 512) for i in range(NQ2)]
        prb2 = [bank(NQ2 + i, 512) for i in range(NQ2)]
        plog = [bank(2 * NQ2 + i, 512, nr=16) for i in range(NQ2)]
        for q2 in range(NQ2):
            mm(pmb2[q2], onesr.v(0, 128), rows.v(q2 * 512, 512), True, True)
            mm(prb2[q2], onesr.v(0, 128), rows.v(1024 + q2 * 512, 512), True, True)
        for kc in range(KC):
            lnf0 = SCA if kc % 2 == 0 else SCB
            for q2 in range(NQ2):
                xv = xT.v(kc * tok + q2 * 512, 512)
                t1 = FAR.v(RBF, 512)
                lvv = FAR.v(lnf0 + q2 * 512, 512)
                tt("vector", t1, xv, pmb2[q2], ALU.subtract)
                tt("vector", lvv, t1, prb2[q2], ALU.mult)
                mm(plog[q2], rwTs.v(kc * nexp, nexp), lvv, kc == 0, kc == KC - 1)
            lnfull = FAR.v(lnf0, tok)
            pch = BAR.v(PCH, tok)
            pcl = BAR.v(PCL, tok)
            split_pair(lnfull, pch, pcl)
            for tc in range(TC):
                for (srcoff, dst) in ((PCH, TMH), (PCL, TML)):
                    ptv = PT.v((tc % 2) * 1024, 128)
                    transpose(ptv, BAR.v(srcoff + tc * 128, 128), identb.v(0, 128))
                    vcopy("vector", BAR.v(dst + tc * d + kc * 128, 128), ptv)
        sp1 = dma_out(dr["tms"][:, 0:TC * d], BAR.v(TMH, TC * d), eng="sync")
        sp2 = dma_out(dr["tms"][:, TC * d:2 * TC * d], BAR.v(TML, TC * d),
                      eng="sync")

        for q2 in range(NQ2):
            vcopy("vector", FAR.v(KARR + q2 * 512, 512, nr=16), plog[q2])
        for tc in range(TC):
            pt = bank(tc % 2, nexp)
            transpose(pt, FAR.v(KARR + tc * 128, 128, nr=16),
                      identf.v(0, 16, nr=16))
            vcopy("vector", logtm.v(tc * nexp, nexp), pt)
        cA, cB, cC, cD = (cols4.v(i, 1) for i in range(4))
        for tc in range(TC):
            lg = logtm.v(tc * nexp, nexp)
            sel = smsk.v(0, nexp)
            selm = smsk.v(nexp, nexp)
            msk1 = smsk.v(2 * nexp, nexp)
            msk2 = smsk.v(3 * nexp, nexp)
            ecol = smsk.v(4 * nexp, nexp)
            tt("vector", sel, lg, biasbc.v(0, nexp), ALU.add)
            red(cA, sel, ALU.max)
            tss("vector", msk1, sel, cA, ALU.is_equal)
            stt("vector", selm, msk1, -1e30, sel, ALU.mult, ALU.add)
            red(cB, selm, ALU.max)
            tss("vector", msk2, selm, cB, ALU.is_equal)
            red(cC, lg, ALU.max)
            tss("vector", cC, cC, -1.0, ALU.mult)
            act(ecol, lg, AF.Exp, bias=cC)
            tt("vector", sel, ecol, msk1, ALU.mult)
            red(cA, sel, ALU.add)
            tt("vector", sel, ecol, msk2, ALU.mult)
            red(cB, sel, ALU.add)
            tt("vector", cD, cA, cB, ALU.add)
            recip(cD, cD)
            tt("vector", cA, cA, cD, ALU.mult)
            tt("vector", cB, cB, cD, ALU.mult)
            cmbv = cmbs.v(tc * nexp, nexp)
            tss("vector", sel, msk2, cB, ALU.mult)
            stt("vector", cmbv, msk1, cA, sel, ALU.mult, ALU.add)
            split_pair(cmbv, cmbh.v(tc * nexp, nexp), cmbl.v(tc * nexp, nexp))
            tt("vector", m12.v(tc * nexp, nexp), msk1, msk2, ALU.add)
        for tc in range(TC):
            pc = bank(2 + tc % 2, nexp)
            for j in range(tc + 1):
                statv = tri2.v(0, 128) if j == tc else ones2d.v(0, 128)
                mm(pc, statv, m12.v(j * nexp, nexp), j == 0, j == tc)
            vcopy("vector", cumcs.v(tc * nexp, nexp), pc)
            tt("vector", slotsS.v(tc * nexp, nexp), m12.v(tc * nexp, nexp),
               cumcs.v(tc * nexp, nexp), ALU.mult)
            tss("vector", slotsS.v(tc * nexp, nexp), slotsS.v(tc * nexp, nexp),
                -1.0, ALU.add)

        NSQ = tok // SQ
        NPC = KC * 256 // 1024
        for sq in range(NSQ):
            ln_qt(sq, mean_r, rstd_r, FMH, FML, w=SQ)
            def sg_dma(gp):
                if gp >= SHC * NPC:
                    return
                arrv = FAR.v(ARR + (gp % 3) * 1024, 1024)
                dma(arrv, dr["susg"][:, gp * 1024:(gp + 1) * 1024])

            def sg_split(gp):
                if gp >= SHC * NPC:
                    return
                slot = SGB + (gp % 2) * 2048
                arrv = FAR.v(ARR + (gp % 3) * 1024, 1024)
                split_pair(arrv, BAR.v(slot, 1024), BAR.v(slot + 1024, 1024),
                           hi_eng="scalar", lo_eng="vector")
            sg_dma(0)
            sg_dma(1)
            sg_split(0)
            for j in range(SHC):
                psu, psg = bank(0, SQ), bank(1, SQ)
                for kc in range(KC):
                    if kc % 4 == 0:
                        p = j * NPC + kc // 4
                        sg_dma(p + 2)
                        sg_split(p + 1)
                    gp = j * NPC + kc // 4
                    slot = SGB + (gp % 2) * 2048
                    o2 = (kc % 4) * 256
                    mh = BAR.v(FMH + kc * SQ, SQ)
                    ml = BAR.v(FML + kc * SQ, SQ)
                    mm3(psu, BAR.v(slot + o2, 128),
                        BAR.v(slot + 1024 + o2, 128), mh, ml,
                        kc == 0, kc == KC - 1)
                    mm3(psg, BAR.v(slot + o2 + 128, 128),
                        BAR.v(slot + 1024 + o2 + 128, 128), mh, ml,
                        kc == 0, kc == KC - 1)
                t1 = FAR.v(SCA, SQ)
                t2 = FAR.v(SCB, SQ)
                act(t1, psu, AF.Silu)
                tt("vector", t2, t1, psg, ALU.mult)
                split_pair(t2, BAR.v(HSH + j * SQ, SQ), BAR.v(HSL + j * SQ, SQ))

            def sd_out(oc, psv, sq=sq):
                xv = xT.v(oc * tok + sq * SQ, SQ)
                tt("vector", xv, psv, xv, ALU.add)
            proj3p(dr["sdT"],
                   lambda kc: BAR.v(HSH + kc * SQ, SQ),
                   lambda kc: BAR.v(HSL + kc * SQ, SQ),
                   sd_out, KC, SHC, SQ, wph=FMH, wpl=FML)

        rl1 = dma(BAR.v(TMH, TC * d), dr["tms"][:, 0:TC * d], eng="sync",
                  dep=[sp1, sp2])
        rl2 = dma(BAR.v(TML, TC * d), dr["tms"][:, TC * d:2 * TC * d],
                  eng="sync", dep=[sp1, sp2])

        for e in range(nexp):
            for tc in range(TC):
                tss("vector", sbfS.v(tc * cap, cap), iotf.v(0, cap),
                    View(slotsS.t_ap[:, tc * nexp + e:tc * nexp + e + 1],
                         slotsS.v(tc * nexp, nexp).bufs), ALU.is_equal)
            for ci, (c0, cw) in enumerate(CCS):
                for tc in range(TC):
                    ptv = PT.v((tc % 2) * 1024, 128, nr=cw)
                    transpose(ptv, View(sbfS.t_ap[:, tc * cap + c0:
                                                  tc * cap + c0 + cw],
                                        sbfS.v(tc * cap, cap).bufs),
                              identb.v(0, 128))
                    vcopy("vector", stm.v(ci * tok + tc * 128, 128, nr=cw), ptv)
                pcc = bank(0 + ci, 1, nr=cw)
                for tc in range(TC):
                    sv = View(sbfS.t_ap[:, tc * cap + c0:tc * cap + c0 + cw],
                              sbfS.v(tc * cap, cap).bufs)
                    ch = View(cmbh.t_ap[:, tc * nexp + e:tc * nexp + e + 1],
                              cmbh.v(tc * nexp, nexp).bufs)
                    cl = View(cmbl.t_ap[:, tc * nexp + e:tc * nexp + e + 1],
                              cmbl.v(tc * nexp, nexp).bufs)
                    mm(pcc, sv, ch, tc == 0, False)
                    mm(pcc, sv, cl, False, tc == TC - 1)
                vcopy("vector", cmbcol.v(ci, 1, nr=cw), pcc)
            nft = d // 512
            for kcg in range(0, KC, 4):
                for kc in range(kcg, min(kcg + 4, KC)):
                    pg = bank(2 + kc - kcg, cap)
                    sv = sbfS
                    for tc in range(TC):
                        th = BAR.v(TMH + tc * d + kc * 128, 128)
                        tl = BAR.v(TML + tc * d + kc * 128, 128)
                        smov = sbfS.v(tc * cap, cap)
                        mm(pg, th, smov, tc == 0, False)
                        mm(pg, tl, smov, False, tc == TC - 1)
                    vcopy("vector", FAR.v(FGT + kc * cap, cap), pg)
            for wname in ("wu", "wg"):
                dramw = dr[wname]
                ngrp = (EFC + 3) // 4
                for g2 in range(ngrp):
                    gw = min(4, EFC - g2 * 4)
                    base = g2 * KC * 512
                    for pc in range(KC // 2):
                        cols = 2 * gw * 128
                        arrv = FAR.v(ARR + (pc % 3) * 1024, cols)
                        dma(arrv, dramw[e, :, base + pc * 2 * gw * 128:
                                        base + (pc + 1) * 2 * gw * 128])
                        for k in range(2):
                            kc = pc * 2 + k
                            for oi in range(gw):
                                sv = FAR.v(ARR + (pc % 3) * 1024 +
                                           (k * gw + oi) * 128, 128)
                                mm(bank(2 + oi, cap), sv, FAR.v(FGT + kc * cap, cap),
                                   kc == 0, kc == KC - 1)
                    for oi in range(gw):
                        efc = g2 * 4 + oi
                        if wname == "wu":
                            vcopy("vector", FAR.v(FU + efc * cap, cap),
                                  bank(2 + oi, cap))
                        else:
                            t1 = FAR.v(SCA, cap)
                            act(t1, FAR.v(FU + efc * cap, cap), AF.Silu)
                            tt("vector", FAR.v(FH + efc * cap, cap), t1,
                               bank(2 + oi, cap), ALU.mult)
            for ftg in range(0, nft, 2):
                nfi = min(2, nft - ftg)
                for kcd in range(EFC):
                    arrv = FAR.v(ARR + (kcd % 3) * 1024, nfi * 512)
                    dma(arrv, dr["wd"][e, :, kcd * d + ftg * 512:
                                       kcd * d + (ftg + nfi) * 512])
                    for fi in range(nfi):
                        for ci, (c0, cw) in enumerate(CCS):
                            pd = bank(2 + fi * 2 + ci, 512, nr=cw)
                            sv = FAR.v(FH + kcd * cap + c0, cw)
                            mv = FAR.v(ARR + (kcd % 3) * 1024 + fi * 512, 512)
                            mm(pd, sv, mv, kcd == 0, kcd == EFC - 1)
                for fi in range(nfi):
                    ft = ftg + fi
                    eo0 = EOS + (ft % 2) * 2048
                    for ci, (c0, cw) in enumerate(CCS):
                        pd = bank(2 + fi * 2 + ci, 512, nr=cw)
                        t1 = FAR.v(SCB + (ci % 2) * 512, 512, nr=cw)
                        tss("vector", t1, pd, cmbcol.v(ci, 1, nr=cw), ALU.mult)
                        split_pair(t1, BAR.v(eo0 + ci * 1024, 512, nr=cw),
                                   BAR.v(eo0 + ci * 1024 + 512, 512, nr=cw))
                    for fl in range(4):
                        fc = ft * 4 + fl
                        for q2 in range(NQ2):
                            psc = bank(q2, 512)
                            first = True
                            for ci, (c0, cw) in enumerate(CCS):
                                eoh = BAR.v(eo0 + ci * 1024 + fl * 128, 128, nr=cw)
                                eol = BAR.v(eo0 + ci * 1024 + 512 + fl * 128,
                                            128, nr=cw)
                                smv = stm.v(ci * tok + q2 * 512, 512, nr=cw)
                                mm(psc, eoh, smv, first, False)
                                mm(psc, eol, smv, False, ci == len(CCS) - 1)
                                first = False
                            xv = xT.v(fc * tok + q2 * 512, 512)
                            tt("vector", xv, psc, xv, ALU.add)

        outs = []
        for c in range(KC * tok // XSTEP):
            outs.append(dma_out(dr["yT"][:, c * XSTEP:(c + 1) * XSTEP],
                                xT.v(c * XSTEP, XSTEP)))
        kb.op("gpsimd", None, dep=outs)

        kb.generate()
    return nc


def prepare_maps(inputs, cfg, ncores):
    tok, d, heads = cfg["tok"], cfg["d"], cfg["heads"]
    nexp, shef, ef = cfg["nexp"], cfg["shef"], cfg["ef"]
    KC = d // 128

    x = np.asarray(inputs["L0_attn_query"], dtype=np.float32)
    kv_k = np.asarray(inputs["L0_attn_kv_k"], dtype=np.float32)
    kv_v = np.asarray(inputs["L0_attn_kv_v"], dtype=np.float32)
    Wq = np.asarray(inputs["L0_attn_Wq"], dtype=np.float32)
    Wo = np.asarray(inputs["L0_attn_Wo"], dtype=np.float32)
    rw = np.asarray(inputs["L0_ffn_router_w"], dtype=np.float32)
    rb = np.asarray(inputs["L0_ffn_router_bias"], dtype=np.float32)
    Wu = np.asarray(inputs["L0_ffn_Wu"], dtype=np.float32)
    Wg = np.asarray(inputs["L0_ffn_Wg"], dtype=np.float32)
    Wd = np.asarray(inputs["L0_ffn_Wd"], dtype=np.float32)
    Su = np.asarray(inputs["L0_ffn_Su"], dtype=np.float32)[0]
    Sg = np.asarray(inputs["L0_ffn_Sg"], dtype=np.float32)[0]
    Sd = np.asarray(inputs["L0_ffn_Sd"], dtype=np.float32)[0]

    B, T, D = x.shape
    ntok = B * T
    xf = x.reshape(ntok, D)

    wqT = pack_proj(np.ascontiguousarray(Wq.T), 256)
    woT = pack_proj(np.ascontiguousarray(Wo.T), 256)
    kvt = kv_k.reshape(-1, heads, 128).transpose(1, 2, 0)
    kT = np.ascontiguousarray(np.concatenate(
        [kvt[h] for h in range(heads)], axis=1))
    vvt = kv_v.reshape(-1, heads, 128).transpose(1, 0, 2)
    vv = np.ascontiguousarray(np.concatenate(
        [vvt[h].reshape(-1, 128, 128).transpose(1, 0, 2).reshape(128, -1)
         for h in range(heads)], axis=1))
    rwT = fm(np.ascontiguousarray(rw.T))
    rbr = np.ascontiguousarray(rb.reshape(1, nexp))
    egc = min(4, ef // 128) * 128
    wu_p = np.stack([pack_proj(Wu[e], egc) for e in range(nexp)])
    wg_p = np.stack([pack_proj(Wg[e], egc) for e in range(nexp)])
    wd_p = np.stack([fm(Wd[e]) for e in range(nexp)])
    SHC = shef // 128
    blocks = []
    for j in range(SHC):
        for kc in range(KC):
            blocks.append(np.concatenate(
                [Su[kc * 128:(kc + 1) * 128, j * 128:(j + 1) * 128],
                 Sg[kc * 128:(kc + 1) * 128, j * 128:(j + 1) * 128]], axis=1))
    susg = np.ascontiguousarray(np.concatenate(blocks, axis=1))
    sdT = pack_proj(np.ascontiguousarray(Sd), 256)

    weights = dict(wqT=wqT, woT=woT, kT=kT, vv=vv, rwT=rwT, rb=rbr,
                   wu=wu_p, wg=wg_p, wd=wd_p, susg=susg, sdT=sdT)

    per = ntok // ncores
    in_maps = []
    for c in range(ncores):
        xs = np.ascontiguousarray(xf[c * per:(c + 1) * per].T)
        m = dict(weights)
        m["xT"] = fm(xs)
        in_maps.append(m)
    return in_maps, (B, T, D)


def kernel(**inputs):
    cfg = dict(FULL)
    in_maps, (B, T, D) = prepare_maps(inputs, cfg, NCORES)
    per = B * T // NCORES
    nc = build(cfg)
    res = run_bass_kernel_spmd(nc, in_maps, core_ids=list(range(NCORES)))
    outs = []
    for c in range(NCORES):
        yT = np.asarray(res.results[c]["yT"])
        outs.append(unfm(yT, D, per).T)
    return np.concatenate(outs, axis=0).reshape(B, T, D).astype(np.float32)
